# version 09
# baseline (speedup 1.0000x reference)
import math
from contextlib import ExitStack

import numpy as np
import ml_dtypes

import concourse.bass as bass
import concourse.mybir as mybir
from concourse.bass_utils import run_bass_kernel_spmd

F32 = mybir.dt.float32
BF16 = mybir.dt.bfloat16
AF = mybir.ActivationFunctionType
ALU = mybir.AluOpType
AX = mybir.AxisListType

T = 4096
D = 1024
H = 16
HD = 64
NT = T // 128
NCT = D // 128
FF = 4096
NE = 8
KSEL = T // 2
EPS_LN = 1e-5
EPS_RMS = 1e-6
C_GELU = 0.7978845608028654


class KB:
    ENGINES = ("tensor", "vector", "scalar", "sync", "gpsimd")

    def __init__(self, nc):
        self.nc = nc
        self.q = {e: [] for e in self.ENGINES}
        self.sem = {}
        self.cnt = {}
        self.stack = ExitStack()
        self.phase_stacks = []

    def begin_phase(self):
        self.phase_stacks.append(ExitStack())

    def end_phase(self):
        self.phase_stacks.pop().close()

    def barrier(self):
        snap = [(s, self.cnt[s]) for s in self.sem]
        for e in self.ENGINES:
            self.wait(e, snap)

    def semaphore(self, name):
        self.sem[name] = self.stack.enter_context(self.nc.semaphore(name))
        self.cnt[name] = 0

    def sb(self, name, shape, dtype):
        st = self.phase_stacks[-1] if self.phase_stacks else self.stack
        return st.enter_context(self.nc.sbuf_tensor(name, shape, dtype))

    def sbp(self, name, shape, dtype):
        return self.stack.enter_context(self.nc.sbuf_tensor(name, shape, dtype))

    def ps(self, name, shape, dtype=F32):
        st = self.phase_stacks[-1] if self.phase_stacks else self.stack
        return st.enter_context(self.nc.psum_tensor(name, shape, dtype))

    def n_instr(self):
        return sum(len(q) for q in self.q.values())

    def emit(self, eng, fn, inc=None, waits=(), amt=1):
        tok = None
        if inc is not None:
            self.cnt[inc] += amt
            tok = self.cnt[inc]
        sem = self.sem
        w = [(s, v) for (s, v) in waits if v and v > 0]

        def run(h, fn=fn, w=w, inc_s=inc, amt=amt):
            for s, v in w:
                h.wait_ge(sem[s], v)
            r = fn(h)
            if inc_s is not None:
                r.then_inc(sem[inc_s], amt)
        self.q[eng].append(run)
        return tok

    def dma(self, eng, out, in_, waits=(), ch="d0", **kw):
        if ch not in self.sem:
            self.semaphore(ch)
        tok = self.emit(eng, lambda h: h.dma_start(out=out, in_=in_, **kw),
                        inc=ch, waits=waits, amt=16)
        return (ch, tok)

    def dma_all_done(self):
        return [(s, self.cnt[s]) for s in self.sem if s.startswith("ch_")]

    def wait(self, eng, waits):
        w = [(s, v) for (s, v) in waits if v and v > 0]
        if not w:
            return
        sem = self.sem

        def run(h, w=w):
            for s, v in w:
                h.wait_ge(sem[s], v)
        self.q[eng].append(run)

    ENG_SEM = {"vector": "ve", "scalar": "sc", "tensor": "mm", "gpsimd": "gp"}

    def flush(self, eng):
        return self.emit(eng, lambda h: h.drain(), inc=self.ENG_SEM[eng])

    def finalize(self):
        with self.nc.Block() as block:
            for e in self.ENGINES:
                ops = self.q[e]
                if not ops:
                    continue
                getattr(block, e)(lambda h, ops=ops: [f(h) for f in ops])
        self.stack.close()



def emit_ln_pass(kb, x_src_dram, xnT_dram, xnf_dram, ident_sb, tag, tile_cb=None):
    xin = [kb.sb(f"{tag}_xin{j}", [128, D], F32) for j in range(2)]
    ln = [kb.sb(f"{tag}_ln{j}", [128, D], F32) for j in range(2)]
    lnb = [kb.sb(f"{tag}_lnb{j}", [128, D], BF16) for j in range(2)]
    stg = [[kb.sb(f"{tag}_stg{j}{g}", [128, 512], BF16) for g in range(2)] for j in range(2)]
    xsq = kb.sb(f"{tag}_xsq", [128, D], F32)
    msum = kb.sb(f"{tag}_msum", [128, 1], F32)
    sqsum = kb.sb(f"{tag}_sqsum", [128, 1], F32)
    ms = kb.sb(f"{tag}_ms", [128, 1], F32)
    msq = kb.sb(f"{tag}_msq", [128, 1], F32)
    var = kb.sb(f"{tag}_var", [128, 1], F32)
    rstd = kb.sb(f"{tag}_rstd", [128, 1], F32)
    eps5 = kb.sb(f"{tag}_eps5", [128, 1], F32)
    pt = [kb.ps(f"{tag}_pt{j}", [128, 512], BF16) for j in range(2)]

    tok_eps = kb.emit("vector", lambda v: v.memset(eps5[:, :], EPS_LN), inc="ve")

    xin_free_ve = [0, 0]
    xin_free_sc = [0, 0]
    ln_free_st = [(f"ch_{tag}lnst{j}", 0) for j in range(2)]
    lnb_free_mm = [0, 0]
    ev_done = [0, 0]
    stg_free = [[(f"ch_{tag}stg{j}{g}", 0) for g in range(2)] for j in range(2)]
    prev_var = 0
    prev_apply = 0

    for i in range(NT):
        b = i % 2
        tld = kb.dma("sync", xin[b][:, :], x_src_dram[i * 128:(i + 1) * 128, :],
                     waits=[("ve", xin_free_ve[b]), ("sc", xin_free_sc[b])],
                     ch=f"ch_{tag}xin{b}")
        w_ld = [tld]
        t_red = kb.emit("vector", lambda v, b=b: v.reduce_sum(msum[:, :], xin[b][:, :], axis=AX.X),
                        inc="ve", waits=w_ld)
        t_sq = kb.emit("scalar", lambda s, b=b: s.activation(xsq[:, :], xin[b][:, :], AF.Square,
                                                             accum_out=sqsum[:, :]),
                       inc="sc", waits=w_ld + [("ve", prev_var)])
        xin_free_sc[b] = t_sq
        kb.emit("vector", lambda v: v.tensor_scalar(ms[:, :], msum[:, :], 1.0 / D, None, op0=ALU.mult),
                inc="ve", waits=[("ve", t_red), ("sc", t_sq)])
        t_ms = kb.cnt["ve"]
        kb.emit("vector", lambda v: v.tensor_tensor(msq[:, :], ms[:, :], ms[:, :], op=ALU.mult),
                inc="ve", waits=[("ve", t_ms)])
        t_msq = kb.cnt["ve"]
        t_var = kb.emit("vector", lambda v: v.scalar_tensor_tensor(var[:, :], sqsum[:, :], 1.0 / D,
                                                                   msq[:, :], op0=ALU.mult,
                                                                   op1=ALU.subtract),
                        inc="ve", waits=[("ve", t_msq)])
        prev_var = t_var
        kb.emit("scalar", lambda s: s.activation(rstd[:, :], var[:, :], AF.Ln, bias=eps5[:, 0:1]),
                waits=[("ve", t_var), ("ve", tok_eps), ("ve", prev_apply)])
        t_rstd = kb.emit("scalar", lambda s: s.activation(rstd[:, :], rstd[:, :], AF.Exp, scale=-0.5),
                         inc="sc")
        t_ln = kb.emit("vector", lambda v, b=b: v.tensor_scalar(ln[b][:, :], xin[b][:, :], ms[:, 0:1],
                                                                rstd[:, 0:1], op0=ALU.subtract,
                                                                op1=ALU.mult),
                       inc="ve", waits=[("sc", t_rstd), ln_free_st[b]])
        prev_apply = t_ln
        xin_free_ve[b] = t_ln
        t_lnb = kb.emit("vector", lambda v, b=b: v.tensor_copy(lnb[b][:, :], ln[b][:, :]),
                        inc="ve", waits=[("mm", lnb_free_mm[b])])
        if xnf_dram is not None:
            t_fl = kb.flush("vector")
            ln_free_st[b] = kb.dma("sync", xnf_dram[i * 128:(i + 1) * 128, :], ln[b][:, :],
                                   waits=[("ve", t_fl)], ch=f"ch_{tag}lnst{b}")
        for g in range(2):
            for k in range(4):
                kk = g * 4 + k
                kb.emit("tensor",
                        lambda t, b=b, kk=kk, k=k: t.transpose(
                            pt[b][:, k * 128:(k + 1) * 128], lnb[b][:, kk * 128:(kk + 1) * 128],
                            ident_sb[:, :]),
                        inc=("mm" if k == 3 else None),
                        waits=([("ve", t_lnb), ("ve", ev_done[b])] if k == 0 else []))
            t_tr = kb.cnt["mm"]
            if g == 1:
                lnb_free_mm[b] = t_tr
            t_ev = kb.emit("vector", lambda v, b=b, g=g: v.tensor_copy(stg[b][g][:, :], pt[b][:, :]),
                           inc="ve", waits=[("mm", t_tr), stg_free[b][g]])
            ev_done[b] = t_ev
            t_fl = kb.flush("vector")
            last = 0
            for k in range(4):
                kk = g * 4 + k
                last = kb.dma("sync", xnT_dram[kk * 128:(kk + 1) * 128, i * 128:(i + 1) * 128],
                              stg[b][g][:, k * 128:(k + 1) * 128],
                              waits=[("ve", t_fl)] if k == 0 else [], ch=f"ch_{tag}stg{b}{g}")
            stg_free[b][g] = last
        if tile_cb is not None:
            tile_cb(kb, i, ln[b], t_ln)



def phase_A(kb, io):
    wrb = kb.sb("A_wrb", [128, D], F32)
    onesr = kb.sb("A_onesr", [1, 128], F32)
    ones1 = kb.sb("A_ones1", [128, 1], F32)
    scores = io["scores_sb"]
    prodt = kb.sb("A_prod", [128, D], F32)
    pb = kb.ps("A_pb", [128, 512], F32)

    t_onesr = kb.emit("vector", lambda v: v.memset(onesr[:, :], 1.0), inc="ve")
    t_ones1 = kb.emit("vector", lambda v: v.memset(ones1[:, :], 1.0), inc="ve")
    t_wr_ld = kb.dma("sync", wrb[0:1, :], io["wr_row"][:, :], ch="ch_Awr")
    t_ev = 0
    for j in range(2):
        t_mm = kb.emit("tensor",
                       lambda t, j=j: t.matmul(pb[:, :], onesr[:, :], wrb[0:1, j * 512:(j + 1) * 512],
                                               start=True, stop=True),
                       inc="mm", waits=[("ve", t_onesr), t_wr_ld, ("ve", t_ev)])
        t_ev = kb.emit("vector", lambda v, j=j: v.tensor_copy(wrb[:, j * 512:(j + 1) * 512], pb[:, :]),
                       inc="ve", waits=[("mm", t_mm)])
    t_wrb = t_ev

    def tile_cb(kb, i, ln_tile, t_ln):
        kb.emit("vector", lambda v, ln_tile=ln_tile, i=i: v.scalar_tensor_tensor(
            prodt[:, :], ln_tile[:, :], 1.0, wrb[:, :], op0=ALU.mult, op1=ALU.mult,
            accum_out=scores[:, i:i + 1]),
            inc="ve", waits=[("ve", t_wrb)])

    emit_ln_pass(kb, io["x0"], io["xn0T_s"], io["xn0f_s"], io["ident_sb"], "A", tile_cb=tile_cb)
    t_scores = kb.cnt["ve"]

    lo = kb.sb("A_lo", [1, 1], F32)
    hi = kb.sb("A_hi", [1, 1], F32)
    mid = kb.sb("A_mid", [1, 1], F32)
    tmp1 = kb.sb("A_tmp1", [1, 1], F32)
    cge = kb.sb("A_cge", [1, 1], F32)
    ncge = kb.sb("A_ncge", [1, 1], F32)
    cnt_sb = kb.sb("A_cnt", [1, 1], F32)
    part = kb.sb("A_part", [128, 1], F32)
    cmp_t = kb.sb("A_cmp", [128, NT], F32)
    thrb = kb.sb("A_thrb", [128, 1], F32)
    sigs = kb.sb("A_sigs", [128, NT], F32)
    pb2 = kb.ps("A_pb2", [128, 512], F32)

    t2 = kb.emit("vector", lambda v: v.reduce_sum(part[:, :], scores[:, :], axis=AX.X,
                                                  apply_absolute_value=True),
                 inc="ve", waits=[("ve", t_scores)])
    t3 = kb.emit("tensor", lambda t: t.matmul(pb2[0:1, 0:1], ones1[:, :], part[:, :],
                                              start=True, stop=True),
                 inc="mm", waits=[("ve", t2), ("ve", t_ones1)])
    t4 = kb.emit("vector", lambda v: v.tensor_scalar(hi[:, :], pb2[0:1, 0:1], 1.0, None, op0=ALU.add),
                 inc="ve", waits=[("mm", t3)])
    t5 = kb.emit("vector", lambda v: v.tensor_scalar(lo[:, :], hi[:, :], -1.0, None, op0=ALU.mult),
                 inc="ve", waits=[("ve", t4)])
    t_mid = kb.emit("vector", lambda v: v.tensor_tensor(mid[:, :], lo[:, :], hi[:, :], op=ALU.add),
                    inc="ve", waits=[("ve", t5)])
    t_mid = kb.emit("vector", lambda v: v.tensor_scalar(mid[:, :], mid[:, :], 0.5, None, op0=ALU.mult),
                    inc="ve", waits=[("ve", t_mid)])

    for it in range(50):
        t_b = kb.emit("tensor", lambda t: t.matmul(pb2[:, 4:5], onesr[:, :], mid[0:1, 0:1],
                                                   start=True, stop=True),
                      inc="mm", waits=[("ve", t_mid)])
        t_c = kb.emit("vector", lambda v: v.tensor_copy(thrb[:, :], pb2[:, 4:5]),
                      inc="ve", waits=[("mm", t_b)])
        t_cmp = kb.emit("vector", lambda v: v.tensor_scalar(cmp_t[:, :], scores[:, :], thrb[:, 0:1],
                                                            None, op0=ALU.is_gt),
                        inc="ve", waits=[("ve", t_c)])
        t_red = kb.emit("vector", lambda v: v.reduce_sum(part[:, :], cmp_t[:, :], axis=AX.X),
                        inc="ve", waits=[("ve", t_cmp)])
        t_cnt = kb.emit("tensor", lambda t: t.matmul(pb2[0:1, 8:9], ones1[:, :], part[:, :],
                                                     start=True, stop=True),
                        inc="mm", waits=[("ve", t_red)])
        t_cc = kb.emit("vector", lambda v: v.tensor_copy(cnt_sb[:, :], pb2[0:1, 8:9]),
                       inc="ve", waits=[("mm", t_cnt)])
        t_ge = kb.emit("vector", lambda v: v.tensor_scalar(cge[:, :], cnt_sb[:, :], KSEL - 0.5, None,
                                                           op0=ALU.is_ge),
                       inc="ve", waits=[("ve", t_cc)])
        t_ng = kb.emit("vector", lambda v: v.tensor_scalar(ncge[:, :], cge[:, :], -1.0, 1.0,
                                                           op0=ALU.mult, op1=ALU.add),
                       inc="ve", waits=[("ve", t_ge)])
        tk = kb.emit("vector", lambda v: v.tensor_tensor(tmp1[:, :], mid[:, :], lo[:, :], op=ALU.subtract),
                     inc="ve", waits=[("ve", t_ng)])
        tk = kb.emit("vector", lambda v: v.tensor_tensor(tmp1[:, :], tmp1[:, :], cge[:, :], op=ALU.mult),
                     inc="ve", waits=[("ve", tk)])
        tk = kb.emit("vector", lambda v: v.tensor_tensor(lo[:, :], lo[:, :], tmp1[:, :], op=ALU.add),
                     inc="ve", waits=[("ve", tk)])
        tk = kb.emit("vector", lambda v: v.tensor_tensor(tmp1[:, :], mid[:, :], hi[:, :], op=ALU.subtract),
                     inc="ve", waits=[("ve", tk)])
        tk = kb.emit("vector", lambda v: v.tensor_tensor(tmp1[:, :], tmp1[:, :], ncge[:, :], op=ALU.mult),
                     inc="ve", waits=[("ve", tk)])
        tk = kb.emit("vector", lambda v: v.tensor_tensor(hi[:, :], hi[:, :], tmp1[:, :], op=ALU.add),
                     inc="ve", waits=[("ve", tk)])
        t_mid = kb.emit("vector", lambda v: v.tensor_tensor(mid[:, :], lo[:, :], hi[:, :], op=ALU.add),
                        inc="ve", waits=[("ve", tk)])
        t_mid = kb.emit("vector", lambda v: v.tensor_scalar(mid[:, :], mid[:, :], 0.5, None, op0=ALU.mult),
                        inc="ve", waits=[("ve", t_mid)])

    t_bl = kb.emit("tensor", lambda t: t.matmul(pb2[:, 12:13], onesr[:, :], lo[0:1, 0:1],
                                                start=True, stop=True),
                   inc="mm", waits=[("ve", t_mid)])
    t_cl = kb.emit("vector", lambda v: v.tensor_copy(thrb[:, :], pb2[:, 12:13]),
                   inc="ve", waits=[("mm", t_bl)])
    t_m = kb.emit("vector", lambda v: v.tensor_scalar(cmp_t[:, :], scores[:, :], thrb[:, 0:1], None,
                                                      op0=ALU.is_gt),
                  inc="ve", waits=[("ve", t_cl)])
    t_sg = kb.emit("scalar", lambda s: s.activation(sigs[:, :], scores[:, :], AF.Sigmoid),
                   inc="sc", waits=[("ve", t_scores)])
    t_w = kb.emit("vector", lambda v: v.tensor_tensor(io["w_sb"][:, :], sigs[:, :], cmp_t[:, :],
                                                      op=ALU.mult),
                  inc="ve", waits=[("ve", t_m), ("sc", t_sg)])
    io["tok_w"] = t_w
    io["tok_scores"] = t_scores



def phase_B(kb, io, dbg=False):
    wqb = kb.sb("B_wqb", [128, 8 * 1024], BF16)
    wkb = kb.sb("B_wkb", [128, 8 * 1024], BF16)
    wvb = kb.sb("B_wvb", [128, 8 * 1024], BF16)
    wob = kb.sb("B_wob", [128, 8 * 1024], BF16)
    wg1b = kb.sb("B_wg1b", [128, 8 * 64], BF16)
    wg2b = kb.sb("B_wg2b", [64, 1024], BF16)
    nw8 = kb.sb("B_nw8", [128, 8], F32)
    wtmp = kb.sb("B_wtmp", [128, 1024], F32)
    eps6 = kb.sb("B_eps6", [128, 1], F32)
    umask = kb.sb("B_umask", [128, 128], F32)
    zeros128 = kb.sb("B_zeros", [128, 128], F32)

    for ct in range(NCT):
        kb.dma("gpsimd", wqb[:, ct * 1024:(ct + 1) * 1024],
               io["wq"][ct * 128:(ct + 1) * 128, :], ch="ch_Bw")
        kb.dma("gpsimd", wkb[:, ct * 1024:(ct + 1) * 1024],
               io["wk"][ct * 128:(ct + 1) * 128, :], ch="ch_Bw")
        kb.dma("gpsimd", wvb[:, ct * 1024:(ct + 1) * 1024],
               io["wv"][ct * 128:(ct + 1) * 128, :], ch="ch_Bw")
        kb.dma("gpsimd", wg1b[:, ct * 64:(ct + 1) * 64],
               io["wg1"][ct * 128:(ct + 1) * 128, :], ch="ch_Bw")
    t_wlast = kb.dma("gpsimd", wg2b[:, :], io["wg2"][:, :], ch="ch_Bw")
    t_nw = kb.dma("sync", nw8[:, :], io["nw8"][:, :], ch="ch_Bnw")
    t_um = kb.dma("sync", umask[:, :], io["umask"][:, :], ch="ch_Bum")
    t_prev = 0
    for ct in range(NCT):
        t_l = kb.dma("sync", wtmp[:, :], io["wo"][ct * 128:(ct + 1) * 128, :],
                     waits=[("ve", t_prev)], ch="ch_Bwt")
        t_prev = kb.emit("vector", lambda v, ct=ct: v.tensor_scalar(
            wob[:, ct * 1024:(ct + 1) * 1024], wtmp[:, :], nw8[:, ct:ct + 1], None, op0=ALU.mult),
            inc="ve", waits=[t_l, t_nw])
    tok_wo = t_prev
    tok_eps6 = kb.emit("vector", lambda v: v.memset(eps6[:, :], EPS_RMS), inc="ve")
    tok_z128 = kb.emit("vector", lambda v: v.memset(zeros128[:, :], 0.0), inc="ve")

    carr = kb.sb("B_carr", [128, 8], F32)
    SCb = kb.sb("B_SCb", [128, 512], BF16)
    SSb = kb.sb("B_SSb", [128, 512], BF16)
    t_init = [kb.emit("vector", lambda v: v.memset(carr[:, :], 1.0), inc="ve")]
    for bufm in (SCb, SSb):
        t_init.append(kb.emit("vector", lambda v, bufm=bufm: v.memset(bufm[:, :], 0.0), inc="ve"))
    tok_init = max(t_init)

    xg = kb.sb("B_xg", [128, 8 * 512], BF16)
    qT = kb.sb("B_qT", [128, 8 * 512], BF16)
    ek = kb.sb("B_ek", [128, 8 * 512], BF16)
    vg = kb.sb("B_vg", [128, 4 * 1024], BF16)
    sgg = kb.sb("B_sg", [128, 4 * 1024], F32)
    xg1 = kb.sb("B_xg1", [64, 512], BF16)
    cosg = kb.sb("B_cos", [128, 4096], BF16)
    sing = kb.sb("B_sin", [128, 4096], BF16)
    zg = kb.sb("B_zg", [128, 1024], F32)
    rz = kb.sb("B_rz", [128, 1024], F32)
    kc2 = [kb.sb(f"B_kc{j}", [128, 1024], BF16) for j in range(2)]
    ksn2 = [kb.sb(f"B_ks{j}", [128, 1024], BF16) for j in range(2)]
    qc2 = [kb.sb(f"B_qc{j}", [128, 1024], BF16) for j in range(2)]
    qs2 = [kb.sb(f"B_qs{j}", [128, 1024], BF16) for j in range(2)]
    tmpf = kb.sb("B_tmpf", [128, 1024], F32)
    kct2 = [kb.sb(f"B_kct{j}", [128, 1024], BF16) for j in range(2)]
    kst2 = [kb.sb(f"B_kst{j}", [128, 1024], BF16) for j in range(2)]
    atm = [kb.sb(f"B_atm{j}", [128, 128], BF16) for j in range(2)]
    gc = kb.sb("B_gc", [128, 1024], F32)
    gsq = kb.sb("B_gsq", [128, 1024], F32)
    rmsq = kb.sb("B_rmsq", [128, 16], F32)
    rstd16 = kb.sb("B_rstd16", [128, 16], F32)
    nrm = kb.sb("B_nrm", [128, 1024], BF16)
    nrmT = kb.sb("B_nrmT", [128, 1024], BF16)
    sstg = kb.sb("B_sstg", [128, 1024], F32)

    pj = [kb.ps(f"B_pj{j}", [128, 512], F32) for j in range(2)]
    ptr = [kb.ps(f"B_ptr{j}", [128, 512], BF16) for j in range(2)]
    pat = [kb.ps(f"B_pat{j}", [128, 512], F32) for j in range(2)]
    pSC = kb.ps("B_pSC", [128, 512], F32)
    pSS = kb.ps("B_pSS", [128, 512], F32)

    pj_free = [("ve", 0), ("ve", 0)]
    ptr_free = [("ve", 0), ("ve", 0)]
    pat_free = [("ve", 0), ("ve", 0)]
    spsum_free = ("ve", 0)
    gd_mm = 0
    gd_ve = 0
    gd_sc = 0
    carr_tok = [tok_init] * NCT
    scb_tok = tok_init
    sstg_free = ("ch_Bs0", 0)
    tok_sq_gc = 0
    chunk_mm_tok = [0, 0]
    nrm_tr_tok = 0

    kb.wait("tensor", [t_um])

    def dump(name, src_ap, waits):
        if dbg and name in io:
            kb.dma("sync", io[name][:, :], src_ap, waits=waits)

    for g in range(8):
        t_xg = [kb.dma("sync", xg[:, ct * 512:(ct + 1) * 512],
                       io["xn0T_s"][ct * 128:(ct + 1) * 128, g * 512:(g + 1) * 512],
                       waits=([("mm", gd_mm)] if ct == 0 else []), ch="ch_Bxg")
                for ct in range(NCT)]
        t_cos = kb.dma("gpsimd", cosg[:, :], io["cosT"][:, g * 4096:(g + 1) * 4096],
                       waits=[("ve", gd_ve)], ch="ch_Bcos")
        t_sin = kb.dma("gpsimd", sing[:, :], io["sinT"][:, g * 4096:(g + 1) * 4096],
                       ch="ch_Bsin")
        t_xgl = t_xg[-1]

        qT_tok = [0] * NCT
        ek_tok = [0] * NCT
        for dt in range(NCT):
            bank = dt % 2
            for j in range(NCT):
                kb.emit("tensor", lambda t, j=j, dt=dt, bank=bank: t.matmul(
                    pj[bank][:, :],
                    wqb[:, j * 1024 + dt * 128: j * 1024 + (dt + 1) * 128],
                    xg[:, j * 512:(j + 1) * 512],
                    start=(j == 0), stop=(j == NCT - 1)),
                    inc=("mm" if j == NCT - 1 else None),
                    waits=([t_xgl, t_wlast, pj_free[bank],
                            ("ve", gd_ve), ("sc", gd_sc)] if j == 0 else []))
            tok = kb.cnt["mm"]
            qT_tok[dt] = kb.emit("scalar", lambda s, dt=dt, bank=bank: s.activation(
                qT[:, dt * 512:(dt + 1) * 512], pj[bank][:, :], AF.Silu),
                inc="sc", waits=[("mm", tok)])
            pj_free[bank] = ("sc", qT_tok[dt])
        for dt in range(NCT):
            bank = dt % 2
            for j in range(NCT):
                kb.emit("tensor", lambda t, j=j, dt=dt, bank=bank: t.matmul(
                    pj[bank][:, :],
                    wkb[:, j * 1024 + dt * 128: j * 1024 + (dt + 1) * 128],
                    xg[:, j * 512:(j + 1) * 512],
                    start=(j == 0), stop=(j == NCT - 1)),
                    inc=("mm" if j == NCT - 1 else None),
                    waits=([pj_free[bank]] if j == 0 else []))
            tok = kb.cnt["mm"]
            ek_tok[dt] = kb.emit("scalar", lambda s, dt=dt, bank=bank: s.activation(
                ek[:, dt * 512:(dt + 1) * 512], pj[bank][:, :], AF.Exp),
                inc="sc", waits=[("mm", tok)])
            pj_free[bank] = ("sc", ek_tok[dt])
        vg_tok = 0
        for m in range(4):
            for dch in range(2):
                bank = (m * 2 + dch) % 2
                for j in range(NCT):
                    kb.emit("tensor", lambda t, j=j, m=m, dch=dch, bank=bank: t.matmul(
                        pj[bank][:, :],
                        xg[:, j * 512 + m * 128: j * 512 + (m + 1) * 128],
                        wvb[:, j * 1024 + dch * 512: j * 1024 + (dch + 1) * 512],
                        start=(j == 0), stop=(j == NCT - 1)),
                        inc=("mm" if j == NCT - 1 else None),
                        waits=([pj_free[bank]] if j == 0 else []))
                tok = kb.cnt["mm"]
                vg_tok = kb.emit("vector", lambda v, m=m, dch=dch, bank=bank: v.tensor_copy(
                    vg[:, m * 1024 + dch * 512: m * 1024 + (dch + 1) * 512], pj[bank][:, :]),
                    inc="ve", waits=[("mm", tok)])
                pj_free[bank] = ("ve", vg_tok)
        if g == 0:
            dump("d_ek", ek[:, :], [("sc", ek_tok[-1])])
            dump("d_qT", qT[:, :], [("sc", qT_tok[-1])])
            dump("d_vg", vg[:, :], [("ve", vg_tok)])
        bank = 0
        for j in range(NCT):
            kb.emit("tensor", lambda t, j=j: t.matmul(
                pj[0][0:64, :], wg1b[:, j * 64:(j + 1) * 64],
                xg[:, j * 512:(j + 1) * 512],
                start=(j == 0), stop=(j == NCT - 1)),
                inc=("mm" if j == NCT - 1 else None),
                waits=([pj_free[0]] if j == 0 else []))
        tok = kb.cnt["mm"]
        t_xg1 = kb.emit("vector", lambda v: v.tensor_copy(xg1[:, :], pj[0][0:64, :]),
                        inc="ve", waits=[("mm", tok)])
        pj_free[0] = ("ve", t_xg1)
        sg_tok = 0
        for m in range(4):
            for dch in range(2):
                bank = (m * 2 + dch) % 2
                tok = kb.emit("tensor", lambda t, m=m, dch=dch, bank=bank: t.matmul(
                    pj[bank][:, :], xg1[0:64, m * 128:(m + 1) * 128],
                    wg2b[0:64, dch * 512:(dch + 1) * 512], start=True, stop=True),
                    inc="mm", waits=[pj_free[bank], ("ve", t_xg1)])
                sg_tok = kb.emit("scalar", lambda s, m=m, dch=dch, bank=bank: s.activation(
                    sgg[:, m * 1024 + dch * 512: m * 1024 + (dch + 1) * 512], pj[bank][:, :],
                    AF.Sigmoid), inc="sc", waits=[("mm", tok)])
                pj_free[bank] = ("sc", sg_tok)

        if g == 0:
            dump("d_sg", sgg[:, :], [("sc", sg_tok)])
        for j4 in range(4):
            c = g * 4 + j4
            cpar = c % 2
            kc = kc2[cpar]
            ksn = ksn2[cpar]
            qc = qc2[cpar]
            qs = qs2[cpar]
            kct = kct2[cpar]
            kst = kst2[cpar]
            rz_tok = [0] * NCT
            for ct in range(NCT):
                t_s = kb.emit("vector", lambda v, ct=ct, j4=j4: v.tensor_tensor_scan(
                    zg[:, ct * 128:(ct + 1) * 128],
                    ek[:, ct * 512 + j4 * 128: ct * 512 + (j4 + 1) * 128],
                    zeros128[:, :], carr[:, ct:ct + 1], op0=ALU.add, op1=ALU.add),
                    inc="ve", waits=[("sc", ek_tok[ct]), ("ve", carr_tok[ct]), ("ve", tok_z128)])
                carr_tok[ct] = kb.emit("vector", lambda v, ct=ct: v.tensor_copy(
                    carr[:, ct:ct + 1], zg[:, ct * 128 + 127: ct * 128 + 128]),
                    inc="ve", waits=[("ve", t_s)])
                rz_tok[ct] = kb.emit("vector", lambda v, ct=ct: v.reciprocal(
                    rz[:, ct * 128:(ct + 1) * 128], zg[:, ct * 128:(ct + 1) * 128]),
                    inc="ve", waits=[("ve", t_s)])
            kq_tok = 0
            for ct in range(NCT):
                co = j4 * 1024 + ct * 128
                so = ct * 512 + j4 * 128
                w0 = [t_cos, t_sin, ("ve", rz_tok[ct]), ("sc", qT_tok[ct]),
                      ("mm", chunk_mm_tok[cpar])]
                kb.emit("vector", lambda v, ct=ct, co=co, so=so, kc=kc: v.tensor_tensor(
                    kc[:, ct * 128:(ct + 1) * 128], ek[:, so:so + 128], cosg[:, co:co + 128],
                    op=ALU.mult), inc="ve", waits=w0)
                kb.emit("vector", lambda v, ct=ct, co=co, so=so, ksn=ksn: v.tensor_tensor(
                    ksn[:, ct * 128:(ct + 1) * 128], ek[:, so:so + 128], sing[:, co:co + 128],
                    op=ALU.mult), inc="ve")
                kb.emit("vector", lambda v, co=co, so=so: v.tensor_tensor(
                    tmpf[:, 0:128], qT[:, so:so + 128], cosg[:, co:co + 128], op=ALU.mult),
                    inc="ve")
                kb.emit("vector", lambda v, ct=ct, qc=qc: v.tensor_tensor(
                    qc[:, ct * 128:(ct + 1) * 128], tmpf[:, 0:128], rz[:, ct * 128:(ct + 1) * 128],
                    op=ALU.mult), inc="ve")
                kb.emit("vector", lambda v, co=co, so=so: v.tensor_tensor(
                    tmpf[:, 128:256], qT[:, so:so + 128], sing[:, co:co + 128], op=ALU.mult),
                    inc="ve")
                kq_tok = kb.emit("vector", lambda v, ct=ct, qs=qs: v.tensor_tensor(
                    qs[:, ct * 128:(ct + 1) * 128], tmpf[:, 128:256], rz[:, ct * 128:(ct + 1) * 128],
                    op=ALU.mult), inc="ve")
            if dbg and g == 0 and j4 == 0:
                dump("d_zg", zg[:, :], [("ve", rz_tok[-1])])
                dump("d_rz", rz[:, :], [("ve", rz_tok[-1])])
                dump("d_kc", kc[:, :], [("ve", kq_tok)])
                dump("d_qc", qc[:, :], [("ve", kq_tok)])
            kt_tok = 0
            for (src, dst) in ((kc, kct), (ksn, kst)):
                for half in range(2):
                    for q4 in range(4):
                        ct = half * 4 + q4
                        kb.emit("tensor", lambda t, src=src, ct=ct, q4=q4, half=half: t.transpose(
                            ptr[half][:, q4 * 128:(q4 + 1) * 128],
                            src[:, ct * 128:(ct + 1) * 128], io["ident_sb"][:, :]),
                            inc=("mm" if q4 == 3 else None),
                            waits=([("ve", kq_tok), ptr_free[half]] if q4 == 0 else []))
                    t_tr = kb.cnt["mm"]
                    tk = kb.emit("vector", lambda v, dst=dst, half=half: v.tensor_copy(
                        dst[:, half * 512:(half + 1) * 512], ptr[half][:, :]),
                        inc="ve", waits=[("mm", t_tr)])
                    ptr_free[half] = ("ve", tk)
                    kt_tok = tk
            gc_toks = []
            new_scb = []
            for h in range(H):
                par = h % 2
                dt = h // 2
                base = par * 64
                bank = h % 2
                kb.emit("tensor", lambda t, base=base, dt=dt, bank=bank, kc=kc, qc=qc: t.matmul(
                    pat[bank][:, 0:128], kc[base:base + 64, dt * 128:(dt + 1) * 128],
                    qc[base:base + 64, dt * 128:(dt + 1) * 128], start=True, stop=False),
                    waits=[("ve", kq_tok), pat_free[bank]])
                t_at = kb.emit("tensor", lambda t, base=base, dt=dt, bank=bank, ksn=ksn, qs=qs: t.matmul(
                    pat[bank][:, 0:128], ksn[base:base + 64, dt * 128:(dt + 1) * 128],
                    qs[base:base + 64, dt * 128:(dt + 1) * 128], start=False, stop=True),
                    inc="mm")
                t_atm = kb.emit("vector", lambda v, bank=bank: v.tensor_tensor(
                    atm[bank][:, :], pat[bank][:, 0:128], umask[:, :], op=ALU.mult),
                    inc="ve", waits=[("mm", t_at)])
                kb.emit("tensor", lambda t, bank=bank, j4=j4, h=h: t.matmul(
                    pat[bank][:, 128:192], atm[bank][:, :],
                    vg[:, j4 * 1024 + h * 64: j4 * 1024 + (h + 1) * 64],
                    start=True, stop=False),
                    waits=[("ve", t_atm), ("ve", vg_tok), ("ve", scb_tok)])
                kb.emit("tensor", lambda t, bank=bank, base=base, dt=dt, qc=qc: t.matmul(
                    pat[bank][:, 128:192], qc[base:base + 64, dt * 128:(dt + 1) * 128],
                    SCb[base:base + 64, dt * 64:(dt + 1) * 64], start=False, stop=False))
                t_o = kb.emit("tensor", lambda t, bank=bank, base=base, dt=dt, qs=qs: t.matmul(
                    pat[bank][:, 128:192], qs[base:base + 64, dt * 128:(dt + 1) * 128],
                    SSb[base:base + 64, dt * 64:(dt + 1) * 64], start=False, stop=True),
                    inc="mm")
                t_gc = kb.emit("vector", lambda v, bank=bank, j4=j4, h=h: v.tensor_tensor(
                    gc[:, h * 64:(h + 1) * 64], pat[bank][:, 128:192],
                    sgg[:, j4 * 1024 + h * 64: j4 * 1024 + (h + 1) * 64], op=ALU.mult),
                    inc="ve", waits=[("mm", t_o), ("sc", sg_tok), ("sc", tok_sq_gc)])
                gc_toks.append(t_gc)
                pat_free[bank] = ("ve", t_gc)
                first = (c == 0 and h == 0)
                kb.emit("tensor", lambda t, base=base, h=h, j4=j4, kct=kct, first=first: t.matmul(
                    pSC[base:base + 64, (h // 2) * 64:(h // 2 + 1) * 64],
                    kct[:, h * 64:(h + 1) * 64],
                    vg[:, j4 * 1024 + h * 64: j4 * 1024 + (h + 1) * 64],
                    start=first, stop=False),
                    waits=[("ve", kt_tok)] + ([spsum_free] if h == 0 else []))
                t_st = kb.emit("tensor", lambda t, base=base, h=h, j4=j4, kst=kst, first=first: t.matmul(
                    pSS[base:base + 64, (h // 2) * 64:(h // 2 + 1) * 64],
                    kst[:, h * 64:(h + 1) * 64],
                    vg[:, j4 * 1024 + h * 64: j4 * 1024 + (h + 1) * 64],
                    start=first, stop=False),
                    inc="mm")
                new_scb.append(t_st)
            t_stl = kb.cnt["mm"]
            t_sc1 = kb.emit("vector", lambda v: v.tensor_copy(SCb[:, :], pSC[:, :]),
                            inc="ve", waits=[("mm", t_stl)])
            t_sc2 = kb.emit("vector", lambda v: v.tensor_copy(SSb[:, :], pSS[:, :]),
                            inc="ve")
            spsum_free = ("ve", t_sc2)
            scb_tok = t_sc2
            t_gc_last = max(gc_toks)
            if dbg and g == 0 and j4 == 0:
                dump("d_kct", kct[:, :], [("ve", kt_tok)])
                dump("d_gc", gc[:, :], [("ve", t_gc_last)])
                dump("d_SCb", SCb[:, :], [("ve", scb_tok)])
            tok_sq_gc = kb.emit("scalar", lambda s: s.activation(gsq[:, :], gc[:, :], AF.Square),
                                inc="sc", waits=[("ve", t_gc_last)])
            red_toks = []
            for h in range(H):
                red_toks.append(kb.emit("vector", lambda v, h=h: v.reduce_sum(
                    rmsq[:, h:h + 1], gsq[:, h * 64:(h + 1) * 64], axis=AX.X),
                    inc="ve", waits=([("sc", tok_sq_gc)] if h == 0 else [])))
            kb.emit("scalar", lambda s: s.activation(rstd16[:, :], rmsq[:, :], AF.Ln,
                                                     bias=eps6[:, 0:1], scale=1.0 / HD),
                    waits=[("ve", red_toks[-1]), ("ve", tok_eps6)])
            t_r16 = kb.emit("scalar", lambda s: s.activation(rstd16[:, :], rstd16[:, :], AF.Exp,
                                                             scale=-0.5), inc="sc")
            nrm_tok = 0
            for h in range(H):
                nrm_tok = kb.emit("vector", lambda v, h=h: v.tensor_scalar(
                    nrm[:, h * 64:(h + 1) * 64], gc[:, h * 64:(h + 1) * 64],
                    rstd16[:, h:h + 1], None, op0=ALU.mult),
                    inc="ve", waits=([("sc", t_r16), ("mm", nrm_tr_tok)] if h == 0 else []))
            nrmT_tok = 0
            for half in range(2):
                for q4 in range(4):
                    ct = half * 4 + q4
                    kb.emit("tensor", lambda t, ct=ct, q4=q4, half=half: t.transpose(
                        ptr[half][:, q4 * 128:(q4 + 1) * 128], nrm[:, ct * 128:(ct + 1) * 128],
                        io["ident_sb"][:, :]),
                        inc=("mm" if q4 == 3 else None),
                        waits=([("ve", nrm_tok), ptr_free[half]] if q4 == 0 else []))
                t_tr = kb.cnt["mm"]
                nrmT_tok = kb.emit("vector", lambda v, half=half: v.tensor_copy(
                    nrmT[:, half * 512:(half + 1) * 512], ptr[half][:, :]),
                    inc="ve", waits=[("mm", t_tr)])
                ptr_free[half] = ("ve", nrmT_tok)
            nrm_tr_tok = kb.cnt["mm"]
            s_ev = 0
            for dch in range(2):
                bank = dch % 2
                for j in range(NCT):
                    kb.emit("tensor", lambda t, j=j, dch=dch, bank=bank: t.matmul(
                        pj[bank][:, :], nrmT[:, j * 128:(j + 1) * 128],
                        wob[:, j * 1024 + dch * 512: j * 1024 + (dch + 1) * 512],
                        start=(j == 0), stop=(j == NCT - 1)),
                        inc=("mm" if j == NCT - 1 else None),
                        waits=([("ve", nrmT_tok), pj_free[bank], ("ve", tok_wo),
                                sstg_free] if j == 0 else []))
                tok = kb.cnt["mm"]
                s_ev = kb.emit("vector", lambda v, dch=dch, bank=bank: v.tensor_copy(
                    sstg[:, dch * 512:(dch + 1) * 512], pj[bank][:, :]),
                    inc="ve", waits=[("mm", tok)])
                pj_free[bank] = ("ve", s_ev)
            t_fl = kb.flush("vector")
            sstg_free = kb.dma("sync", io["s0_s"][c * 128:(c + 1) * 128, :], sstg[:, :],
                               waits=[("ve", t_fl)], ch="ch_Bs0")
            chunk_mm_tok[cpar] = kb.cnt["mm"]
        gd_mm = kb.cnt["mm"]
        gd_ve = kb.cnt["ve"]
        gd_sc = kb.cnt["sc"]
    io["tok_s0_st"] = ("ch_Bs0", kb.cnt["ch_Bs0"])




def phase_C(kb, io):
    xgc = kb.sb("C_xg", [128, 8 * 512], BF16)
    w1b = kb.sb("C_w1b", [128, 8 * 2048], BF16)
    w2b = kb.sb("C_w2b", [128, 16 * 1024], BF16)
    hb = kb.sb("C_hb", [128, 16 * 512], BF16)
    pre_t = [kb.sb(f"C_pre{j}", [128, 512], F32) for j in range(2)]
    sq_t = [kb.sb(f"C_sq{j}", [128, 512], F32) for j in range(2)]
    p_t = [kb.sb(f"C_p{j}", [128, 512], F32) for j in range(2)]
    z_t = [kb.sb(f"C_zz{j}", [128, 512], F32) for j in range(2)]
    sg_t = [kb.sb(f"C_sgm{j}", [128, 512], F32) for j in range(2)]
    b1r = kb.sb("C_b1r", [128, 32], F32)
    b2b = kb.sb("C_b2b", [128, 1024], F32)
    onesr = kb.sb("C_onesr", [1, 128], F32)
    b2row = kb.sb("C_b2row", [1, 1024], F32)
    of_t = [kb.sb(f"C_of{j}", [128, 1024], F32) for j in range(2)]
    xn0_t = [kb.sb(f"C_xn0{j}", [128, 1024], F32) for j in range(2)]
    x0_t = [kb.sb(f"C_x0{j}", [128, 1024], F32) for j in range(2)]
    s0_t = [kb.sb(f"C_s0{j}", [128, 1024], F32) for j in range(2)]
    ya = kb.sb("C_ya", [128, 1024], F32)
    yb = kb.sb("C_yb", [128, 1024], F32)
    yg = kb.sb("C_yg", [128, 1024], F32)
    ystg = [kb.sb(f"C_ystg{j}", [128, 1024], F32) for j in range(2)]

    ph = [kb.ps(f"C_ph{j}", [128, 512], F32) for j in range(2)]
    po = [kb.ps(f"C_po{j}", [128, 512], F32) for j in range(2)]

    t_b1 = kb.dma("sync", b1r[:, :], io["b1r"][:, :])
    t_b2r = kb.dma("sync", b2row[:, :], io["b2row"][:, :])
    t_or = kb.emit("vector", lambda v: v.memset(onesr[:, :], 1.0), inc="ve")
    t_ev = 0
    for j in range(2):
        t_mm = kb.emit("tensor", lambda t, j=j: t.matmul(
            po[0][:, :], onesr[:, :], b2row[0:1, j * 512:(j + 1) * 512], start=True, stop=True),
            inc="mm", waits=[("ve", t_or), t_b2r, ("ve", t_ev)])
        t_ev = kb.emit("vector", lambda v, j=j: v.tensor_copy(b2b[:, j * 512:(j + 1) * 512],
                                                              po[0][:, :]),
                       inc="ve", waits=[("mm", t_mm)])
    tok_b2b = t_ev

    ph_free = [("ve", 0), ("ve", 0)]
    po_free = [("ve", tok_b2b), ("ve", 0)]
    xg_free_mm = 0
    slab_free_mm = 0
    buf_free = {"of": [("ve", 0), ("ve", 0)], "xn0": [("ve", 0), ("ve", 0)],
                "x0": [("ve", 0), ("ve", 0)], "s0": [("ve", 0), ("ve", 0)]}
    ystg_free = [("ch_Cy0", 0), ("ch_Cy1", 0)]
    epi_prev_sc = 0

    gel_prev = [dict(h=0, p=0, sg=0) for _ in range(2)]
    for fh in range(2):
        t_w1 = [kb.dma("gpsimd", w1b[:, ct * 2048:(ct + 1) * 2048],
                       io["W1"][ct * 128:(ct + 1) * 128, fh * 2048:(fh + 1) * 2048],
                       waits=([("mm", slab_free_mm)] if ct == 0 else []), ch="ch_Cw1")
                for ct in range(NCT)]
        t_w2 = [kb.dma("gpsimd", w2b[:, ft * 1024:(ft + 1) * 1024],
                       io["W2"][fh * 2048 + ft * 128: fh * 2048 + (ft + 1) * 128, :],
                       ch="ch_Cw2")
                for ft in range(16)]
        t_wl = t_w2[-1]
        for tc in range(8):
            t_xg = [kb.dma("sync", xgc[:, ct * 512:(ct + 1) * 512],
                           io["xn0T_s"][ct * 128:(ct + 1) * 128, tc * 512:(tc + 1) * 512],
                           waits=([("mm", xg_free_mm)] if ct == 0 else []), ch="ch_Cxg")
                    for ct in range(NCT)]
            t_xgl = t_xg[-1]
            h_toks = []
            for ft in range(16):
                bank = ft % 2
                par = ft % 2
                for j in range(NCT):
                    kb.emit("tensor", lambda t, j=j, ft=ft, bank=bank: t.matmul(
                        ph[bank][:, :],
                        w1b[:, j * 2048 + ft * 128: j * 2048 + (ft + 1) * 128],
                        xgc[:, j * 512:(j + 1) * 512],
                        start=(j == 0), stop=(j == NCT - 1)),
                        inc=("mm" if j == NCT - 1 else None),
                        waits=([t_xgl, t_w1[-1] if ft == 0 else ("mm", 0), ph_free[bank]]
                               if j == 0 else []))
                tok = kb.cnt["mm"]
                fcol = fh * 16 + ft
                gp = gel_prev[par]
                t_pre = kb.emit("scalar", lambda s, bank=bank, fcol=fcol, par=par: s.activation(
                    pre_t[par][:, :], ph[bank][:, :], AF.Identity, bias=b1r[:, fcol:fcol + 1]),
                    inc="sc", waits=[("mm", tok), t_b1, ("ve", gp["h"])])
                ph_free[bank] = ("sc", t_pre)
                t_sq = kb.emit("scalar", lambda s, par=par: s.activation(
                    sq_t[par][:, :], pre_t[par][:, :], AF.Square),
                    inc="sc", waits=[("ve", gp["p"])])
                t_p = kb.emit("vector", lambda v, par=par: v.tensor_scalar(
                    p_t[par][:, :], sq_t[par][:, :], 2 * C_GELU * 0.044715, 2 * C_GELU,
                    op0=ALU.mult, op1=ALU.add), inc="ve", waits=[("sc", t_sq)])
                t_z = kb.emit("vector", lambda v, par=par: v.tensor_tensor(
                    z_t[par][:, :], p_t[par][:, :], pre_t[par][:, :], op=ALU.mult), inc="ve",
                    waits=[("ve", t_p), ("sc", gp["sg"])])
                t_sg = kb.emit("scalar", lambda s, par=par: s.activation(
                    sg_t[par][:, :], z_t[par][:, :], AF.Sigmoid),
                    inc="sc", waits=[("ve", t_z), ("ve", gp["h"])])
                t_h = kb.emit("vector", lambda v, ft=ft, par=par: v.tensor_tensor(
                    hb[:, ft * 512:(ft + 1) * 512], sg_t[par][:, :], pre_t[par][:, :],
                    op=ALU.mult),
                    inc="ve", waits=[("sc", t_sg)])
                gel_prev[par] = dict(h=t_h, p=t_p, sg=t_sg)
                h_toks.append(t_h)
            t_hl = h_toks[-1]
            for m in range(4):
                i = tc * 4 + m
                bb = m % 2
                t_ge = 0
                for dc in range(2):
                    bank = dc % 2
                    for ft in range(16):
                        kb.emit("tensor", lambda t, ft=ft, m=m, dc=dc, bank=bank: t.matmul(
                            po[bank][:, :],
                            hb[:, ft * 512 + m * 128: ft * 512 + (m + 1) * 128],
                            w2b[:, ft * 1024 + dc * 512: ft * 1024 + (dc + 1) * 512],
                            start=(ft == 0), stop=(ft == 15)),
                            inc=("mm" if ft == 15 else None),
                            waits=([("ve", t_hl), t_wl, po_free[bank]] if ft == 0 else []))
                    tok = kb.cnt["mm"]
                    if fh == 0:
                        tk = kb.emit("vector", lambda v, bb=bb, dc=dc, bank=bank: v.tensor_copy(
                            of_t[bb][:, dc * 512:(dc + 1) * 512], po[bank][:, :]),
                            inc="ve", waits=[("mm", tok), buf_free["of"][bb]])
                        po_free[bank] = ("ve", tk)
                        if dc == 1:
                            t_fl = kb.flush("vector")
                            kb.dma("sync", io["ofh_s"][i * 128:(i + 1) * 128, :],
                                   of_t[bb][:, :], waits=[("ve", t_fl)], ch="ch_Cofs")
                            buf_free["of"][bb] = ("ve", tk)
                    else:
                        if dc == 0:
                            t_of = kb.dma("sync", of_t[bb][:, :],
                                          io["ofh_s"][i * 128:(i + 1) * 128, :],
                                          waits=[buf_free["of"][bb],
                                                 ("ch_Cofs", kb.cnt.get("ch_Cofs", 0))],
                                          ch=f"ch_Cof{bb}")
                        tk = kb.emit("vector", lambda v, bb=bb, dc=dc, bank=bank: v.tensor_tensor(
                            ya[:, dc * 512:(dc + 1) * 512], of_t[bb][:, dc * 512:(dc + 1) * 512],
                            po[bank][:, :], op=ALU.add),
                            inc="ve", waits=[("mm", tok), t_of])
                        po_free[bank] = ("ve", tk)
                if fh == 1:
                    t_ya = tk
                    buf_free["of"][bb] = ("ve", t_ya)
                    if "ypre_s" in io:
                        t_fl = kb.flush("vector")
                        kb.dma("sync", io["ypre_s"][i * 128:(i + 1) * 128, :], ya[:, :],
                               waits=[("ve", t_fl)], ch="ch_dump")
                    t_xn = kb.dma("sync", xn0_t[bb][:, :], io["xn0f_s"][i * 128:(i + 1) * 128, :],
                                  waits=[buf_free["xn0"][bb]], ch=f"ch_Cxn{bb}")
                    t_x0 = kb.dma("sync", x0_t[bb][:, :], io["x0"][i * 128:(i + 1) * 128, :],
                                  waits=[buf_free["x0"][bb]], ch=f"ch_Cx0{bb}")
                    t_s0 = kb.dma("sync", s0_t[bb][:, :], io["s0_s"][i * 128:(i + 1) * 128, :],
                                  waits=[buf_free["s0"][bb]], ch=f"ch_Cs0{bb}")
                    t1 = kb.emit("vector", lambda v: v.tensor_tensor(
                        yb[:, :], ya[:, :], b2b[:, :], op=ALU.add),
                        inc="ve", waits=[("ve", t_ya), ("ve", tok_b2b), ("sc", epi_prev_sc)])
                    t2 = kb.emit("vector", lambda v, i=i: v.tensor_scalar(
                        yg[:, :], yb[:, :], io["w_sb"][:, i:i + 1], None, op0=ALU.mult),
                        inc="ve", waits=[("ve", io["tok_w"])])
                    t3 = kb.emit("vector", lambda v, bb=bb: v.tensor_tensor(
                        yb[:, :], yg[:, :], xn0_t[bb][:, :], op=ALU.add),
                        inc="ve", waits=[t_xn])
                    buf_free["xn0"][bb] = ("ve", t3)
                    t4 = kb.emit("scalar", lambda s: s.activation(yg[:, :], yb[:, :], AF.Silu),
                                 inc="sc", waits=[("ve", t3)])
                    epi_prev_sc = t4
                    t5 = kb.emit("vector", lambda v, bb=bb: v.tensor_tensor(
                        yb[:, :], s0_t[bb][:, :], yg[:, :], op=ALU.mult),
                        inc="ve", waits=[("sc", t4), t_s0])
                    buf_free["s0"][bb] = ("ve", t5)
                    t6 = kb.emit("vector", lambda v, bb=bb, m=m: v.tensor_tensor(
                        ystg[m % 2][:, :], x0_t[bb][:, :], yb[:, :], op=ALU.add),
                        inc="ve", waits=[("ve", t5), t_x0, ystg_free[m % 2]])
                    buf_free["x0"][bb] = ("ve", t6)
                    t_fl = kb.flush("vector")
                    ystg_free[m % 2] = kb.dma("sync", io["x1_s"][i * 128:(i + 1) * 128, :],
                                              ystg[m % 2][:, :], waits=[("ve", t_fl)],
                                              ch=f"ch_Cy{m % 2}")
            xg_free_mm = kb.cnt["mm"]
        slab_free_mm = kb.cnt["mm"]
    io["tok_x1_st"] = ("ch_Dnxin", 0)



def phase_D(kb, io):
    gwb = kb.sb("D_gwb", [128, 8 * 1024], F32)
    gbb = kb.sb("D_gbb", [128, 8], F32)
    onesr = kb.sb("D_onesr", [1, 128], F32)
    growt = kb.sb("D_grow", [1, 8 * 1024], F32)
    gbrow = kb.sb("D_gbrow", [1, 8], F32)
    prodt = kb.sb("D_prod", [128, D], F32)
    lg8 = kb.sb("D_lg8", [128, 8], F32)
    lgb = kb.sb("D_lgb", [128, 8], F32)
    srt = kb.sb("D_srt", [128, 8], F32)
    d12 = kb.sb("D_d12", [128, 1], F32)
    w1n = kb.sb("D_w1n", [128, 1], F32)
    w2n = kb.sb("D_w2n", [128, 1], F32)
    eq1 = kb.sb("D_eq1", [128, 8], F32)
    eq2 = kb.sb("D_eq2", [128, 8], F32)
    c1 = kb.sb("D_c1", [128, 8], F32)
    pd = kb.ps("D_pd", [128, 512], F32)

    t_gw = kb.dma("sync", growt[:, :], io["gwT"][:, :], ch="ch_Dgw")
    t_gb = kb.dma("sync", gbrow[:, :], io["gbrow"][:, :], ch="ch_Dgb")
    t_or = kb.emit("vector", lambda v: v.memset(onesr[:, :], 1.0), inc="ve")
    t_ev = 0
    for e in range(NE):
        for j in range(2):
            t_mm = kb.emit("tensor", lambda t, e=e, j=j: t.matmul(
                pd[:, :], onesr[:, :],
                growt[0:1, e * 1024 + j * 512: e * 1024 + (j + 1) * 512],
                start=True, stop=True),
                inc="mm", waits=[("ve", t_or), t_gw, ("ve", t_ev)])
            t_ev = kb.emit("vector", lambda v, e=e, j=j: v.tensor_copy(
                gwb[:, e * 1024 + j * 512: e * 1024 + (j + 1) * 512], pd[:, :]),
                inc="ve", waits=[("mm", t_mm)])
    t_mm = kb.emit("tensor", lambda t: t.matmul(pd[:, 0:8], onesr[:, :], gbrow[0:1, :],
                                                start=True, stop=True),
                   inc="mm", waits=[t_gb, ("ve", t_ev)])
    t_gbb = kb.emit("vector", lambda v: v.tensor_copy(gbb[:, :], pd[:, 0:8]),
                    inc="ve", waits=[("mm", t_mm)])

    def tile_cb(kb, i, ln_tile, t_ln):
        red_toks = []
        for e in range(NE):
            red_toks.append(kb.emit("vector", lambda v, ln_tile=ln_tile, e=e: v.scalar_tensor_tensor(
                prodt[:, :], ln_tile[:, :], 1.0, gwb[:, e * 1024:(e + 1) * 1024],
                op0=ALU.mult, op1=ALU.mult, accum_out=lg8[:, e:e + 1]),
                inc="ve", waits=([("ve", t_gbb)] if e == 0 else [])))
        tk = kb.emit("vector", lambda v: v.tensor_tensor(lgb[:, :], lg8[:, :], gbb[:, :],
                                                         op=ALU.add),
                     inc="ve", waits=[("ve", red_toks[-1])])
        tk = kb.emit("vector", lambda v: v.max(srt[:, :], lgb[:, :]), inc="ve",
                     waits=[("ve", tk)])
        tk = kb.emit("vector", lambda v: v.tensor_tensor(d12[:, :], srt[:, 0:1], srt[:, 1:2],
                                                         op=ALU.subtract),
                     inc="ve", waits=[("ve", tk)])
        t_w1n = kb.emit("scalar", lambda s: s.activation(w1n[:, :], d12[:, :], AF.Sigmoid),
                        inc="sc", waits=[("ve", tk)])
        t_w2n = kb.emit("vector", lambda v: v.tensor_scalar(w2n[:, :], w1n[:, :], -1.0, 1.0,
                                                            op0=ALU.mult, op1=ALU.add),
                        inc="ve", waits=[("sc", t_w1n)])
        t_e1 = kb.emit("vector", lambda v: v.tensor_scalar(eq1[:, :], lgb[:, :], srt[:, 0:1],
                                                           None, op0=ALU.is_equal),
                       inc="ve", waits=[("ve", t_w2n)])
        t_e2 = kb.emit("vector", lambda v: v.tensor_scalar(eq2[:, :], lgb[:, :], srt[:, 1:2],
                                                           None, op0=ALU.is_equal),
                       inc="ve", waits=[("ve", t_e1)])
        t_c1 = kb.emit("vector", lambda v: v.tensor_scalar(c1[:, :], eq1[:, :], w1n[:, 0:1],
                                                           None, op0=ALU.mult),
                       inc="ve", waits=[("ve", t_e2)])
        t_c2 = kb.emit("vector", lambda v, i=i: v.tensor_scalar(
            eq1[:, :], eq2[:, :], w2n[:, 0:1], None, op0=ALU.mult),
            inc="ve", waits=[("ve", t_c1)])
        kb.emit("vector", lambda v, i=i: v.tensor_tensor(
            io["comb_sb"][:, i * 8:(i + 1) * 8], c1[:, :], eq1[:, :], op=ALU.add),
            inc="ve", waits=[("ve", t_c2)])

    emit_ln_pass(kb, io["x1_s"], io["xn1T_s"], None, io["ident_sb"], "Dn", tile_cb=tile_cb)
    io["tok_comb"] = kb.cnt["ve"]
    if "comb_dump" in io:
        t_fl = kb.flush("vector")
        kb.dma("sync", io["comb_dump"][:, :], io["comb_sb"][:, :],
               waits=[("ve", t_fl)], ch="ch_dump")
    io["tok_xn1_st"] = ("ve", kb.cnt["ve"])




def _e1(kb, io):
    wqb = kb.sb("E_wqb", [128, 8 * 1024], BF16)
    wkb = kb.sb("E_wkb", [128, 8 * 1024], BF16)
    wvb = kb.sb("E_wvb", [128, 8 * 1024], BF16)
    xg = kb.sb("E_xg", [128, 8 * 512], BF16)
    stg = [kb.sb(f"E_stg{j}", [128, 512], BF16) for j in range(2)]
    pj = [kb.ps(f"E_pj{j}", [128, 512], F32) for j in range(2)]

    for ct in range(NCT):
        kb.dma("gpsimd", wqb[:, ct * 1024:(ct + 1) * 1024],
               io["l1wq"][ct * 128:(ct + 1) * 128, :], ch="ch_Ew")
        kb.dma("gpsimd", wkb[:, ct * 1024:(ct + 1) * 1024],
               io["l1wk"][ct * 128:(ct + 1) * 128, :], ch="ch_Ew")
        t_wl = kb.dma("gpsimd", wvb[:, ct * 1024:(ct + 1) * 1024],
                      io["l1wv"][ct * 128:(ct + 1) * 128, :], ch="ch_Ew")

    pj_free = [("ve", 0), ("ve", 0)]
    stg_free = [("ve", 0), ("ve", 0)]
    stg_dma = [("ch_Est0", 0), ("ch_Est1", 0)]
    xg_free_mm = 0
    for g in range(8):
        t_xg = [kb.dma("sync", xg[:, ct * 512:(ct + 1) * 512],
                       io["xn1T_s"][ct * 128:(ct + 1) * 128, g * 512:(g + 1) * 512],
                       waits=([("mm", xg_free_mm)] if ct == 0 else []), ch="ch_Exg")
                for ct in range(NCT)]
        t_xgl = t_xg[-1]
        for (wb, dst, chn) in ((wqb, io["QT_s"], "q"), (wkb, io["KT_s"], "k")):
            for dt in range(NCT):
                bank = dt % 2
                sb = dt % 2
                for j in range(NCT):
                    kb.emit("tensor", lambda t, j=j, dt=dt, bank=bank, wb=wb: t.matmul(
                        pj[bank][:, :],
                        wb[:, j * 1024 + dt * 128: j * 1024 + (dt + 1) * 128],
                        xg[:, j * 512:(j + 1) * 512],
                        start=(j == 0), stop=(j == NCT - 1)),
                        inc=("mm" if j == NCT - 1 else None),
                        waits=([t_xgl, t_wl, pj_free[bank], stg_free[sb], stg_dma[sb]]
                               if j == 0 else []))
                tok = kb.cnt["mm"]
                t_ev = kb.emit("vector", lambda v, sb=sb, bank=bank: v.tensor_copy(
                    stg[sb][:, :], pj[bank][:, :]), inc="ve", waits=[("mm", tok)])
                pj_free[bank] = ("ve", t_ev)
                stg_free[sb] = ("ve", t_ev)
                t_fl = kb.flush("vector")
                stg_dma[sb] = kb.dma("sync", dst[dt * 128:(dt + 1) * 128, g * 512:(g + 1) * 512],
                                     stg[sb][:, :], waits=[("ve", t_fl)], ch=f"ch_Est{sb}")
                yield
        for m in range(4):
            for dch in range(2):
                bank = (m * 2 + dch) % 2
                sb = (m * 2 + dch) % 2
                for j in range(NCT):
                    kb.emit("tensor", lambda t, j=j, m=m, dch=dch, bank=bank: t.matmul(
                        pj[bank][:, :],
                        xg[:, j * 512 + m * 128: j * 512 + (m + 1) * 128],
                        wvb[:, j * 1024 + dch * 512: j * 1024 + (dch + 1) * 512],
                        start=(j == 0), stop=(j == NCT - 1)),
                        inc=("mm" if j == NCT - 1 else None),
                        waits=([pj_free[bank], stg_free[sb], stg_dma[sb]] if j == 0 else []))
                tok = kb.cnt["mm"]
                t_ev = kb.emit("vector", lambda v, sb=sb, bank=bank: v.tensor_copy(
                    stg[sb][:, :], pj[bank][:, :]), inc="ve", waits=[("mm", tok)])
                pj_free[bank] = ("ve", t_ev)
                stg_free[sb] = ("ve", t_ev)
                t_fl = kb.flush("vector")
                stg_dma[sb] = kb.dma("sync",
                                     io["V_s"][(g * 4 + m) * 128:(g * 4 + m + 1) * 128,
                                               dch * 512:(dch + 1) * 512],
                                     stg[sb][:, :], waits=[("ve", t_fl)], ch=f"ch_Est{sb}")
                yield
        xg_free_mm = kb.cnt["mm"]


def _e2(kb, io):
    qhp = [kb.sb(f"E_qhp{j}", [128, 4096], BF16) for j in range(2)]
    khp = [kb.sb(f"E_khp{j}", [128, 4096], BF16) for j in range(2)]
    vh = [kb.sb(f"E_vh{j}", [128, 32 * 64], BF16) for j in range(2)]
    vex = [kb.sb(f"E_vex{j}", [128, 32 * 65], BF16) for j in range(2)]
    pt2 = [kb.sb(f"E_pt{i}", [128, 1024], BF16) for i in range(2)]
    oc = [kb.sb(f"E_oc{j}", [65, 512], F32) for j in range(2)]
    rs = [kb.sb(f"E_rs{j}", [1, 512], F32) for j in range(2)]
    obf = [kb.sb(f"E_obf{j}", [64, 512], BF16) for j in range(2)]
    onesr = kb.sb("E_onesr", [1, 64], F32)
    pS = [kb.ps(f"E_pS{i}", [128, 1024], F32) for i in range(2)]
    pO = [kb.ps(f"E_pO{j}", [128, 512], F32) for j in range(2)]

    t_or = kb.emit("vector", lambda v: v.memset(onesr[:, :], 1.0), inc="ve")
    hload_free = [("mm", 0), ("mm", 0)]
    vex_free = [("mm", 0), ("mm", 0)]
    pS_free = [("sc", 0)] * 2
    pt_free = [("mm", 0)] * 2
    pO_free = [("ve", 0), ("ve", 0)]
    oc_free = [("ve", 0), ("ve", 0)]
    rs_free = [("mm", 0), ("mm", 0)]
    obf_dma = [("ch_Eo0", 0), ("ch_Eo1", 0)]
    for hp in range(H // 2):
        pb_ = hp % 2
        t_vex = [0, 0]
        t_q = kb.dma("sync", qhp[pb_][:, :], io["QT_s"][hp * 128:(hp + 1) * 128, :],
                     waits=[hload_free[pb_]], ch=f"ch_Eqh{pb_}")
        t_k = kb.dma("sync", khp[pb_][:, :], io["KT_s"][hp * 128:(hp + 1) * 128, :],
                     ch=f"ch_Ekh{pb_}")
        t_ld = [None, None]
        for j in range(2):
            h = hp * 2 + j
            t_v = [kb.dma("sync", vh[j][:, kt * 64:(kt + 1) * 64],
                          io["V_s"][kt * 128:(kt + 1) * 128, h * 64:(h + 1) * 64],
                          ch=f"ch_Evh{j}")
                   for kt in range(NT)]
            t_ld[j] = t_v[-1]
            t_ms = kb.emit("vector", lambda v, j=j: v.memset(vex[j][:, :], 1.0),
                           inc="ve", waits=[vex_free[j], t_q, t_k, t_v[-1]])
            t_cp = 0
            for kt in range(NT):
                t_cp = kb.emit("vector", lambda v, j=j, kt=kt: v.tensor_copy(
                    vex[j][:, kt * 65:kt * 65 + 64], vh[j][:, kt * 64:(kt + 1) * 64]),
                    inc="ve", waits=([("ve", t_ms)] if kt == 0 else []))
            t_vex[j] = t_cp
        for qc in range(8):
            exp_toks = [0] * NT
            for kt in range(NT + 2):
                if kt < NT:
                    pi = kt % 2
                    for j in range(2):
                        kb.emit("tensor", lambda t, j=j, kt=kt, qc=qc, pi=pi, pb_=pb_: t.matmul(
                            pS[pi][:, j * 512:(j + 1) * 512],
                            khp[pb_][j * 64:(j + 1) * 64, kt * 128:(kt + 1) * 128],
                            qhp[pb_][j * 64:(j + 1) * 64, qc * 512:(qc + 1) * 512],
                            start=True, stop=True),
                            inc=("mm" if j == 1 else None),
                            waits=([pS_free[pi]] if j == 0 else [])
                                  + ([t_q, t_k, t_ld[0], t_ld[1],
                                      ("ve", t_vex[0]), ("ve", t_vex[1])]
                                     if kt == 0 and qc == 0 else []))
                    t_st = kb.cnt["mm"]
                    exp_toks[kt] = kb.emit("scalar", lambda s, pi=pi: s.activation(
                        pt2[pi][:, :], pS[pi][:, :], AF.Exp, scale=0.125),
                        inc="sc", waits=[("mm", t_st), pt_free[pi]])
                    pS_free[pi] = ("sc", exp_toks[kt])
                if kt >= 2:
                    ko = kt - 2
                    pi = ko % 2
                    for j in range(2):
                        kb.emit("tensor", lambda t, j=j, ko=ko, pi=pi: t.matmul(
                            pO[j][0:65, :], vex[j][:, ko * 65:(ko + 1) * 65],
                            pt2[pi][:, j * 512:(j + 1) * 512],
                            start=(ko == 0), stop=(ko == NT - 1)),
                            inc=("mm" if j == 1 else None),
                            waits=([("sc", exp_toks[ko])] if j == 0 else [])
                                  + ([pO_free[j]] if ko == 0 else []))
                    pt_free[pi] = ("mm", kb.cnt["mm"])
                yield
            t_av = kb.cnt["mm"]
            for j in range(2):
                h = hp * 2 + j
                t_cpo = kb.emit("vector", lambda v, j=j: v.tensor_copy(oc[j][:, :], pO[j][0:65, :]),
                                inc="ve", waits=[("mm", t_av), oc_free[j]])
                t_rs = kb.emit("vector", lambda v, j=j: v.reciprocal(rs[j][:, :], oc[j][64:65, :]),
                               inc="ve", waits=[("ve", t_cpo), rs_free[j]])
                t_bc = kb.emit("tensor", lambda t, j=j: t.matmul(
                    pO[j][0:64, :], onesr[:, :], rs[j][0:1, :], start=True, stop=True),
                    inc="mm", waits=[("ve", t_rs), ("ve", t_or)])
                rs_free[j] = ("mm", t_bc)
                t_om = kb.emit("vector", lambda v, j=j: v.tensor_tensor(
                    obf[j][:, :], oc[j][0:64, :], pO[j][0:64, :], op=ALU.mult),
                    inc="ve", waits=[("mm", t_bc), obf_dma[j]])
                pO_free[j] = ("ve", t_om)
                oc_free[j] = ("ve", t_om)
                t_fl = kb.flush("vector")
                obf_dma[j] = kb.dma("sync",
                                    io["oT_s"][h * 64:(h + 1) * 64, qc * 512:(qc + 1) * 512],
                                    obf[j][:, :], waits=[("ve", t_fl)], ch=f"ch_Eo{j}")
                yield
        for j in range(2):
            hload_free[j] = ("mm", kb.cnt["mm"])
            vex_free[j] = ("mm", kb.cnt["mm"])


def _e3(kb, io):
    wob = kb.sb("E3_wob", [128, 8 * 1024], BF16)
    bob = kb.sb("E3_bob", [128, 1024], F32)
    borow = kb.sb("E3_borow", [1, 1024], F32)
    onesr3 = kb.sb("E3_onesr", [1, 128], F32)
    ot = [kb.sb(f"E3_ot{j}", [128, 8 * 128], BF16) for j in range(2)]
    sstg = [kb.sb(f"E3_sstg{j}", [128, 1024], F32) for j in range(2)]
    pj3 = [kb.ps(f"E3_pj{j}", [128, 512], F32) for j in range(2)]

    for ct in range(NCT):
        t_wo = kb.dma("gpsimd", wob[:, ct * 1024:(ct + 1) * 1024],
                      io["l1wo"][ct * 128:(ct + 1) * 128, :], ch="ch_E3w")
    t_bo = kb.dma("sync", borow[:, :], io["borow"][:, :], ch="ch_E3b")
    t_or3 = kb.emit("vector", lambda v: v.memset(onesr3[:, :], 1.0), inc="ve")
    t_ev = 0
    for j in range(2):
        t_mm = kb.emit("tensor", lambda t, j=j: t.matmul(
            pj3[0][:, :], onesr3[:, :], borow[0:1, j * 512:(j + 1) * 512], start=True, stop=True),
            inc="mm", waits=[("ve", t_or3), t_bo, ("ve", t_ev)])
        t_ev = kb.emit("vector", lambda v, j=j: v.tensor_copy(bob[:, j * 512:(j + 1) * 512],
                                                              pj3[0][:, :]),
                       inc="ve", waits=[("mm", t_mm)])
    tok_bob = t_ev
    pj3_free = [("ve", tok_bob), ("ve", 0)]
    ot_free = [("mm", 0), ("mm", 0)]
    s_dma = [("ch_Es10", 0), ("ch_Es11", 0)]
    for i in range(NT):
        ib = i % 2
        t_ot = [kb.dma("sync", ot[ib][:, ct * 128:(ct + 1) * 128],
                       io["oT_s"][ct * 128:(ct + 1) * 128, i * 128:(i + 1) * 128],
                       waits=([ot_free[ib]] if ct == 0 else []), ch=f"ch_Eot{ib}")
                for ct in range(NCT)]
        t_se = 0
        for dch in range(2):
            bank = dch % 2
            for j in range(NCT):
                kb.emit("tensor", lambda t, j=j, dch=dch, bank=bank, ib=ib: t.matmul(
                    pj3[bank][:, :], ot[ib][:, j * 128:(j + 1) * 128],
                    wob[:, j * 1024 + dch * 512: j * 1024 + (dch + 1) * 512],
                    start=(j == 0), stop=(j == NCT - 1)),
                    inc=("mm" if j == NCT - 1 else None),
                    waits=([t_ot[-1], t_wo, pj3_free[bank], s_dma[ib]] if j == 0 else []))
            tok = kb.cnt["mm"]
            t_se = kb.emit("vector", lambda v, dch=dch, bank=bank, ib=ib: v.tensor_tensor(
                sstg[ib][:, dch * 512:(dch + 1) * 512], bob[:, dch * 512:(dch + 1) * 512],
                pj3[bank][:, :], op=ALU.add),
                inc="ve", waits=[("mm", tok), ("ve", tok_bob)])
            pj3_free[bank] = ("ve", t_se)
        ot_free[ib] = ("mm", kb.cnt["mm"])
        t_fl = kb.flush("vector")
        s_dma[ib] = kb.dma("sync", io["s1_s"][i * 128:(i + 1) * 128, :], sstg[ib][:, :],
                           waits=[("ve", t_fl)], ch=f"ch_Es1{ib}")
        yield


def gen_E(kb, io):
    kb.begin_phase()
    yield from _e1(kb, io)
    g1 = [("mm", kb.cnt["mm"]), ("ve", kb.cnt["ve"]),
          ("ch_Est0", kb.cnt.get("ch_Est0", 0)), ("ch_Est1", kb.cnt.get("ch_Est1", 0))]
    kb.end_phase()
    kb.begin_phase()
    for e in KB.ENGINES:
        kb.wait(e, g1)
    yield from _e2(kb, io)
    g2 = [("mm", kb.cnt["mm"]), ("ve", kb.cnt["ve"]),
          ("ch_Eo0", kb.cnt.get("ch_Eo0", 0)), ("ch_Eo1", kb.cnt.get("ch_Eo1", 0))]
    kb.end_phase()
    kb.begin_phase()
    for e in KB.ENGINES:
        kb.wait(e, g2)
    yield from _e3(kb, io)
    kb.end_phase()



def phase_G_alloc(kb, io):
    g = {}
    g["w1q"] = [kb.sb(f"G_w1q{j}", [128, 8 * 1024], BF16) for j in range(2)]
    g["w3q"] = [kb.sb(f"G_w3q{j}", [128, 8 * 1024], BF16) for j in range(2)]
    g["w2q"] = [kb.sb(f"G_w2q{j}", [128, 8 * 1024], BF16) for j in range(2)]
    g["xq"] = [kb.sb(f"G_xq{j}", [128, 8 * 512], BF16) for j in range(2)]
    g["h1s"] = [kb.sb(f"G_h1s{j}", [128, 512], BF16) for j in range(2)]
    g["hq"] = kb.sb("G_hq", [128, 8 * 512], BF16)
    g["gstg"] = [kb.sb(f"G_gstg{j}", [128, 1024], F32) for j in range(2)]
    g["ph"] = [kb.ps(f"G_ph{j}", [128, 512], F32) for j in range(2)]

    def load_slab(p, waits_mm):
        e, fq = divmod(p, 4)
        sl = p % 2
        for ct in range(NCT):
            kb.dma("gpsimd", g["w1q"][sl][:, ct * 1024:(ct + 1) * 1024],
                   io["l1w1"][e * D + ct * 128: e * D + (ct + 1) * 128,
                              fq * 1024:(fq + 1) * 1024],
                   waits=([("mm", waits_mm)] if ct == 0 else []), ch="ch_Gw")
        for ct in range(NCT):
            kb.dma("gpsimd", g["w3q"][sl][:, ct * 1024:(ct + 1) * 1024],
                   io["l1w3"][e * D + ct * 128: e * D + (ct + 1) * 128,
                              fq * 1024:(fq + 1) * 1024], ch="ch_Gw")
        last = None
        for ft in range(NCT):
            last = kb.dma("gpsimd", g["w2q"][sl][:, ft * 1024:(ft + 1) * 1024],
                          io["l1w2"][e * FF + fq * 1024 + ft * 128:
                                     e * FF + fq * 1024 + (ft + 1) * 128, :],
                          ch="ch_Gw")
        return last

    def load_xq(p, tc):
        xb = (p * 8 + tc) % 2
        return [kb.dma("sync", g["xq"][xb][:, ct * 512:(ct + 1) * 512],
                       io["xn1T_s"][ct * 128:(ct + 1) * 128, tc * 512:(tc + 1) * 512],
                       waits=([("mm", g["xq_free"][xb])] if ct == 0 else []), ch=f"ch_Gxq{xb}")
                for ct in range(NCT)][-1]

    g["load_slab"] = load_slab
    g["load_xq"] = load_xq
    g["slab_used_mm"] = [0, 0]
    g["slab_tok"] = [None, None]
    g["xq_free"] = [0, 0]
    g["xq_tok"] = [None, None]
    g["bank_free"] = [("ve", 0), ("ve", 0)]
    g["gel_prev"] = [0, 0]
    g["gs_free"] = [("ve", 0), ("ve", 0)]
    g["slab_tok"][0] = load_slab(0, 0)
    g["xq_tok"][0] = load_xq(0, 0)
    return g


def _g_pass(kb, io, g, p, final):
    NPASS = NE * 4
    e, fq = divmod(p, 4)
    sl = p % 2
    if p + 1 < NPASS:
        g["slab_tok"][(p + 1) % 2] = g["load_slab"](p + 1, g["slab_used_mm"][(p + 1) % 2])
    for tc in range(8):
        xb = (p * 8 + tc) % 2
        t_xql = g["xq_tok"][xb]
        nxt = p * 8 + tc + 1
        if nxt < NPASS * 8:
            g["xq_tok"][nxt % 2] = g["load_xq"](nxt // 8, nxt % 8)
        t_hl = 0
        for ft in range(NCT):
            par = ft % 2
            for j in range(NCT):
                kb.emit("tensor", lambda t, j=j, ft=ft, sl=sl, xb=xb: t.matmul(
                    g["ph"][0][:, :],
                    g["w1q"][sl][:, j * 1024 + ft * 128: j * 1024 + (ft + 1) * 128],
                    g["xq"][xb][:, j * 512:(j + 1) * 512],
                    start=(j == 0), stop=(j == NCT - 1)),
                    inc=("mm" if j == NCT - 1 else None),
                    waits=([t_xql, g["slab_tok"][sl], g["bank_free"][0]] if j == 0 else []))
            tokh1 = kb.cnt["mm"]
            t_si = kb.emit("scalar", lambda s, par=par: s.activation(
                g["h1s"][par][:, :], g["ph"][0][:, :], AF.Silu),
                inc="sc", waits=[("mm", tokh1), ("ve", g["gel_prev"][par])])
            g["bank_free"][0] = ("sc", t_si)
            for j in range(NCT):
                kb.emit("tensor", lambda t, j=j, ft=ft, sl=sl, xb=xb: t.matmul(
                    g["ph"][1][:, :],
                    g["w3q"][sl][:, j * 1024 + ft * 128: j * 1024 + (ft + 1) * 128],
                    g["xq"][xb][:, j * 512:(j + 1) * 512],
                    start=(j == 0), stop=(j == NCT - 1)),
                    inc=("mm" if j == NCT - 1 else None),
                    waits=([g["bank_free"][1]] if j == 0 else []))
            tokh3 = kb.cnt["mm"]
            t_h = kb.emit("vector", lambda v, ft=ft, par=par: v.tensor_tensor(
                g["hq"][:, ft * 512:(ft + 1) * 512], g["h1s"][par][:, :], g["ph"][1][:, :],
                op=ALU.mult),
                inc="ve", waits=[("mm", tokh3), ("sc", t_si)])
            g["bank_free"][1] = ("ve", t_h)
            g["gel_prev"][par] = t_h
            t_hl = t_h
            yield
        for m in range(4):
            i = tc * 4 + m
            bb = m % 2
            t_ge = 0
            for dch in range(2):
                bank = dch % 2
                for ft in range(NCT):
                    kb.emit("tensor", lambda t, ft=ft, m=m, dch=dch, bank=bank, sl=sl: t.matmul(
                        g["ph"][bank][:, :],
                        g["hq"][:, ft * 512 + m * 128: ft * 512 + (m + 1) * 128],
                        g["w2q"][sl][:, ft * 1024 + dch * 512: ft * 1024 + (dch + 1) * 512],
                        start=(ft == 0), stop=(ft == NCT - 1)),
                        inc=("mm" if ft == NCT - 1 else None),
                        waits=([("ve", t_hl), g["bank_free"][bank]] if ft == 0 else []))
                tok = kb.cnt["mm"]
                t_ge = kb.emit("vector", lambda v, i=i, e=e, dch=dch, bank=bank, bb=bb: v.tensor_scalar(
                    g["gstg"][bb][:, dch * 512:(dch + 1) * 512], g["ph"][bank][:, :],
                    io["comb_sb"][:, i * 8 + e: i * 8 + e + 1], None, op0=ALU.mult),
                    inc="ve", waits=[("mm", tok), ("ve", io["tok_comb"]), g["gs_free"][bb]])
                g["bank_free"][bank] = ("ve", t_ge)
            if not final:
                t_fl = kb.flush("vector")
                kb.dma("gpsimd", io["gacc_s"][i * 128:(i + 1) * 128, :], g["gstg"][bb][:, :],
                       waits=[("ve", t_fl)],
                       **({"accum_op": ALU.add} if p > 0 else {}), ch="ch_Gacc")
                g["gs_free"][bb] = ("ch_Gacc", kb.cnt["ch_Gacc"])
                yield
            else:
                ep = g["ep"]
                t_gl = kb.dma("sync", ep["gl"][bb][:, :], io["gacc_s"][i * 128:(i + 1) * 128, :],
                              waits=[ep["gl_free"][bb], ("ch_Gacc", kb.cnt.get("ch_Gacc", 0))],
                              ch=f"ch_Ggl{bb}")
                t_x1 = kb.dma("sync", ep["x1t"][bb][:, :], io["x1_s"][i * 128:(i + 1) * 128, :],
                              waits=[ep["x1_free"][bb]], ch=f"ch_Gx1{bb}")
                t_s1 = kb.dma("sync", ep["s1t"][bb][:, :], io["s1_s"][i * 128:(i + 1) * 128, :],
                              waits=[ep["s1_free"][bb]], ch=f"ch_Gs1{bb}")
                t1 = kb.emit("vector", lambda v, bb=bb, ep=ep: v.tensor_tensor(
                    ep["ga"][:, :], g["gstg"][bb][:, :], ep["gl"][bb][:, :], op=ALU.add),
                    inc="ve", waits=[("ve", t_ge), t_gl, ("sc", ep["sc"])])
                g["gs_free"][bb] = ("ve", t1)
                ep["gl_free"][bb] = ("ve", t1)
                t2 = kb.emit("scalar", lambda s, ep=ep: s.activation(
                    ep["gb"][:, :], ep["ga"][:, :], AF.Silu),
                    inc="sc", waits=[("ve", t1)])
                ep["sc"] = t2
                t3 = kb.emit("vector", lambda v, bb=bb, ep=ep: v.tensor_tensor(
                    ep["ga"][:, :], ep["s1t"][bb][:, :], ep["gb"][:, :], op=ALU.mult),
                    inc="ve", waits=[("sc", t2), t_s1])
                ep["s1_free"][bb] = ("ve", t3)
                t4 = kb.emit("vector", lambda v, bb=bb, ep=ep: v.tensor_tensor(
                    ep["x2s"][bb][:, :], ep["x1t"][bb][:, :], ep["ga"][:, :], op=ALU.add),
                    inc="ve", waits=[("ve", t3), t_x1, ep["x2_free"][bb]])
                ep["x1_free"][bb] = ("ve", t4)
                t_fl = kb.flush("vector")
                ep["x2_free"][bb] = kb.dma("sync", io["out"][i * 128:(i + 1) * 128, :],
                                           ep["x2s"][bb][:, :], waits=[("ve", t_fl)],
                                           ch=f"ch_Gout{bb}")
        g["slab_used_mm"][sl] = kb.cnt["mm"]
        g["xq_free"][xb] = kb.cnt["mm"]


def gen_G_main(kb, io, g):
    for p in range(NE * 4 - 1):
        yield from _g_pass(kb, io, g, p, final=False)


def phase_G_final(kb, io, g):
    kb.begin_phase()
    ep = {}
    ep["gl"] = [kb.sb(f"G_gl{j}", [128, 1024], F32) for j in range(2)]
    ep["x1t"] = [kb.sb(f"G_x1t{j}", [128, 1024], F32) for j in range(2)]
    ep["s1t"] = [kb.sb(f"G_s1t{j}", [128, 1024], F32) for j in range(2)]
    ep["ga"] = kb.sb("G_ga", [128, 1024], F32)
    ep["gb"] = kb.sb("G_gb", [128, 1024], F32)
    ep["x2s"] = [kb.sb(f"G_x2s{j}", [128, 1024], F32) for j in range(2)]
    ep["gl_free"] = [("ve", 0), ("ve", 0)]
    ep["x1_free"] = [("ve", 0), ("ve", 0)]
    ep["s1_free"] = [("ve", 0), ("ve", 0)]
    ep["x2_free"] = [("ch_Gout0", 0), ("ch_Gout1", 0)]
    ep["sc"] = 0
    g["ep"] = ep
    for _ in _g_pass(kb, io, g, NE * 4 - 1, final=True):
        pass
    kb.end_phase()



def host_tables():
    chan = np.arange(D, dtype=np.float64)
    theta = 10000.0 ** (-2.0 * chan / D)
    t_idx = np.arange(T, dtype=np.float64)
    ang = np.outer(theta, t_idx)
    cosT = np.cos(ang)
    sinT = np.sin(ang)

    def lay(tab):
        tt = tab.reshape(NCT, 128, NT, 128)
        tt = np.transpose(tt, (1, 2, 0, 3))
        return np.ascontiguousarray(tt.reshape(128, NT * NCT * 128).astype(np.float32))

    u = np.triu(np.ones((128, 128), dtype=np.float32))
    ident = np.eye(128, dtype=ml_dtypes.bfloat16)
    return lay(cosT), lay(sinT), u, ident



def build_program(phases=("A", "B"), debug_outputs=()):
    nc = bass.Bass("TRN2", target_bir_lowering=False)
    io = {}
    if "G" in phases:
        io["out"] = nc.dram_tensor("out", [T, D], F32, kind="ExternalOutput").ap()

    def ext(name, shape, dtype):
        io[name] = nc.dram_tensor(name, shape, dtype, kind="ExternalInput").ap()

    ext("x0", [T, D], F32)
    ext("wr_row", [1, D], F32)
    ext("ident", [128, 128], BF16)
    ext("umask", [128, 128], F32)
    ext("cosT", [128, NT * NCT * 128], F32)
    ext("sinT", [128, NT * NCT * 128], F32)
    ext("wq", [D, D], F32)
    ext("wk", [D, D], F32)
    ext("wv", [D, D], F32)
    ext("wo", [D, D], F32)
    ext("wg1", [D, HD], F32)
    ext("wg2", [HD, D], F32)
    ext("nw8", [128, 8], F32)
    ext("W1", [D, FF], F32)
    ext("l1wq", [D, D], F32)
    ext("l1wk", [D, D], F32)
    ext("l1wv", [D, D], F32)
    ext("l1wo", [D, D], F32)
    ext("borow", [1, D], F32)
    ext("l1w1", [NE * D, FF], F32)
    ext("l1w3", [NE * D, FF], F32)
    ext("l1w2", [NE * FF, D], F32)
    ext("W2", [FF, D], F32)
    ext("b1r", [128, 32], F32)
    ext("b2row", [1, D], F32)
    ext("gwT", [1, NE * D], F32)
    ext("gbrow", [1, NE], F32)

    def scratch(name, shape, dtype):
        kind = "ExternalOutput" if name in debug_outputs else "Internal"
        io[name] = nc.dram_tensor(name, shape, dtype, kind=kind).ap()

    scratch("xn0T_s", [D, T], BF16)
    scratch("xn0f_s", [T, D], F32)
    scratch("s0_s", [T, D], F32)
    scratch("ofh_s", [T, D], F32)
    scratch("QT_s", [D, T], BF16)
    scratch("KT_s", [D, T], BF16)
    scratch("V_s", [T, D], BF16)
    scratch("oT_s", [D, T], BF16)
    scratch("s1_s", [T, D], F32)
    scratch("gacc_s", [T, D], F32)
    if "ypre_s" in debug_outputs:
        io["ypre_s"] = nc.dram_tensor("ypre_s", [T, D], F32, kind="ExternalOutput").ap()
    scratch("x1_s", [T, D], F32)
    scratch("xn1T_s", [D, T], BF16)

    kb = KB(nc)
    for s in ("mm", "ve", "sc", "gp", "lw"):
        kb.semaphore(s)

    io["ident_sb"] = kb.sb("identsb", [128, 128], BF16)
    t_id = kb.dma("sync", io["ident_sb"][:, :], io["ident"][:, :], ch="ch_id")
    kb.wait("tensor", [t_id])
    io["w_sb"] = kb.sb("w_sb", [128, NT], F32)
    io["scores_sb"] = kb.sb("A_scores", [128, NT], F32)
    io["comb_sb"] = kb.sb("comb_sb", [128, NT * 8], F32)
    io["tok_w"] = 0

    if "A" in phases:
        kb.begin_phase()
        phase_A(kb, io)
        kb.end_phase()
        io["tok_xn0_st"] = ("ch_Astg11", kb.cnt.get("ch_Astg11", 0))
        if "w_dump" in debug_outputs:
            io["w_dump"] = nc.dram_tensor("w_dump", [128, NT], F32, kind="ExternalOutput").ap()
            io["scores_dump"] = nc.dram_tensor("scores_dump", [128, NT], F32,
                                               kind="ExternalOutput").ap()
            t_fl = kb.flush("vector")
            kb.dma("sync", io["w_dump"][:, :], io["w_sb"][:, :],
                   waits=[("ve", t_fl)], ch="ch_dump")
            kb.dma("sync", io["scores_dump"][:, :], io["scores_sb"][:, :],
                   waits=[("ve", t_fl)], ch="ch_dump")

    if "B" in phases:
        kb.barrier()
        dbgB = "d_ek" in debug_outputs
        if dbgB:
            for nm, shp, dt_ in [("d_ek", [128, 4096], BF16), ("d_qT", [128, 4096], BF16),
                                 ("d_vg", [128, 4096], BF16), ("d_sg", [128, 4096], F32),
                                 ("d_zg", [128, 1024], F32), ("d_rz", [128, 1024], F32),
                                 ("d_kc", [128, 1024], BF16), ("d_qc", [128, 1024], BF16),
                                 ("d_kct", [128, 1024], BF16), ("d_gc", [128, 1024], F32),
                                 ("d_SCb", [128, 512], BF16)]:
                io[nm] = nc.dram_tensor(nm, shp, dt_, kind="ExternalOutput").ap()

        kb.begin_phase()
        phase_B(kb, io, dbg=dbgB)
        kb.end_phase()

    if "C" in phases:
        kb.barrier()
        kb.begin_phase()
        phase_C(kb, io)
        kb.end_phase()

    if "D" in phases:
        if "comb_dump" in debug_outputs:
            io["comb_dump"] = nc.dram_tensor("comb_dump", [128, NT * 8], F32,
                                             kind="ExternalOutput").ap()
        kb.barrier()
        kb.begin_phase()
        phase_D(kb, io)
        kb.end_phase()

    if "E" in phases and "G" in phases:
        kb.barrier()
        kb.begin_phase()
        g = phase_G_alloc(kb, io)
        ge = gen_E(kb, io)
        gg = gen_G_main(kb, io, g)
        alive_e, alive_g = True, True
        ce = cg = 0
        while alive_e or alive_g:
            if alive_e and (not alive_g or ce * 5 <= cg):
                before = kb.n_instr()
                try:
                    next(ge)
                except StopIteration:
                    alive_e = False
                ce += kb.n_instr() - before
            else:
                before = kb.n_instr()
                try:
                    next(gg)
                except StopIteration:
                    alive_g = False
                cg += kb.n_instr() - before
        kb.barrier()
        phase_G_final(kb, io, g)
        kb.end_phase()
    elif "E" in phases:
        kb.barrier()
        kb.begin_phase()
        for _ in gen_E(kb, io):
            pass
        kb.end_phase()

    final = [(s, kb.cnt[s]) for s in list(kb.sem) if s.startswith("ch_")]
    kb.wait("sync", final)
    kb.wait("gpsimd", final)
    kb.finalize()
    return nc, io


def make_in_map(inputs, core):
    cosl, sinl, u, ident = host_tables()
    nw = np.asarray(inputs["L0_attn_norm_w"], dtype=np.float32)
    m = {
        "x0": np.ascontiguousarray(np.asarray(inputs["L0_attn_x"], dtype=np.float32)[core]),
        "wr_row": np.ascontiguousarray(np.asarray(inputs["L0_ffn_w_router"], dtype=np.float32)[None, :]),
        "ident": ident,
        "umask": u,
        "cosT": cosl,
        "sinT": sinl,
        "wq": np.ascontiguousarray(np.asarray(inputs["L0_attn_wq"], dtype=np.float32)),
        "wk": np.ascontiguousarray(np.asarray(inputs["L0_attn_wk"], dtype=np.float32)),
        "wv": np.ascontiguousarray(np.asarray(inputs["L0_attn_wv"], dtype=np.float32)),
        "wo": np.ascontiguousarray(np.asarray(inputs["L0_attn_wo"], dtype=np.float32)),
        "wg1": np.ascontiguousarray(np.asarray(inputs["L0_attn_wg1"], dtype=np.float32)),
        "wg2": np.ascontiguousarray(np.asarray(inputs["L0_attn_wg2"], dtype=np.float32)),
        "nw8": np.ascontiguousarray(nw.reshape(8, 128).T),
        "W1": np.ascontiguousarray(np.asarray(inputs["L0_ffn_W1"], dtype=np.float32)),
        "W2": np.ascontiguousarray(np.asarray(inputs["L0_ffn_W2"], dtype=np.float32)),
        "b1r": np.ascontiguousarray(np.asarray(inputs["L0_ffn_b1"], dtype=np.float32).reshape(32, 128).T),
        "b2row": np.ascontiguousarray(np.asarray(inputs["L0_ffn_b2"], dtype=np.float32)[None, :]),
        "gwT": np.ascontiguousarray(np.asarray(inputs["L1_ffn_gate_w"], dtype=np.float32).T.reshape(1, -1)),
        "gbrow": np.ascontiguousarray(np.asarray(inputs["L1_ffn_gate_b"], dtype=np.float32)[None, :]),
        "l1wq": np.ascontiguousarray(np.asarray(inputs["L1_attn_Wq"], dtype=np.float32)),
        "l1wk": np.ascontiguousarray(np.asarray(inputs["L1_attn_Wk"], dtype=np.float32)),
        "l1wv": np.ascontiguousarray(np.asarray(inputs["L1_attn_Wv"], dtype=np.float32)),
        "l1wo": np.ascontiguousarray(np.asarray(inputs["L1_attn_Wo"], dtype=np.float32)),
        "borow": np.ascontiguousarray(np.asarray(inputs["L1_attn_bo"], dtype=np.float32)[None, :]),
        "l1w1": np.ascontiguousarray(np.asarray(inputs["L1_ffn_w1"], dtype=np.float32).reshape(NE * D, FF)),
        "l1w3": np.ascontiguousarray(np.asarray(inputs["L1_ffn_w3"], dtype=np.float32).reshape(NE * D, FF)),
        "l1w2": np.ascontiguousarray(np.asarray(inputs["L1_ffn_w2"], dtype=np.float32).reshape(NE * FF, D)),
    }
    return m


_NC_CACHE = {}


def kernel(**inputs):
    key = "full"
    if key not in _NC_CACHE:
        _NC_CACHE[key] = build_program(phases=("A", "B", "C", "D", "E", "G"))
    nc, io = _NC_CACHE[key]
    in_maps = [make_in_map(inputs, c) for c in range(8)]
    res = run_bass_kernel_spmd(nc, in_maps, core_ids=list(range(8)))
    out = np.stack([np.asarray(res.results[c]["out"], dtype=np.float32) for c in range(8)],
                   axis=0)
    return out


# version 11 (final)
# speedup vs baseline: 1.0881x; 1.0881x over previous
import math
from contextlib import ExitStack

import numpy as np
import ml_dtypes

import concourse.bass as bass
import concourse.mybir as mybir
from concourse.bass_utils import run_bass_kernel_spmd

F32 = mybir.dt.float32
BF16 = mybir.dt.bfloat16
AF = mybir.ActivationFunctionType
ALU = mybir.AluOpType
AX = mybir.AxisListType

T = 4096
D = 1024
H = 16
HD = 64
NT = T // 128
NCT = D // 128
FF = 4096
NE = 8
KSEL = T // 2
EPS_LN = 1e-5
EPS_RMS = 1e-6
C_GELU = 0.7978845608028654


class KB:
    ENGINES = ("tensor", "vector", "scalar", "sync", "gpsimd")

    def __init__(self, nc):
        self.nc = nc
        self.q = {e: [] for e in self.ENGINES}
        self.sem = {}
        self.cnt = {}
        self.stack = ExitStack()
        self.phase_stacks = []

    def begin_phase(self):
        self.phase_stacks.append(ExitStack())

    def end_phase(self):
        self.phase_stacks.pop().close()

    def barrier(self):
        snap = [(s, self.cnt[s]) for s in self.sem]
        for e in self.ENGINES:
            self.wait(e, snap)

    def semaphore(self, name):
        self.sem[name] = self.stack.enter_context(self.nc.semaphore(name))
        self.cnt[name] = 0

    def sb(self, name, shape, dtype):
        st = self.phase_stacks[-1] if self.phase_stacks else self.stack
        return st.enter_context(self.nc.sbuf_tensor(name, shape, dtype))

    def sbp(self, name, shape, dtype):
        return self.stack.enter_context(self.nc.sbuf_tensor(name, shape, dtype))

    def ps(self, name, shape, dtype=F32):
        st = self.phase_stacks[-1] if self.phase_stacks else self.stack
        return st.enter_context(self.nc.psum_tensor(name, shape, dtype))

    def n_instr(self):
        return sum(len(q) for q in self.q.values())

    def emit(self, eng, fn, inc=None, waits=(), amt=1):
        tok = None
        if inc is not None:
            self.cnt[inc] += amt
            tok = self.cnt[inc]
        sem = self.sem
        w = [(s, v) for (s, v) in waits if v and v > 0]

        def run(h, fn=fn, w=w, inc_s=inc, amt=amt):
            for s, v in w:
                h.wait_ge(sem[s], v)
            r = fn(h)
            if inc_s is not None:
                r.then_inc(sem[inc_s], amt)
        self.q[eng].append(run)
        return tok

    def dma(self, eng, out, in_, waits=(), ch="d0", **kw):
        if ch not in self.sem:
            self.semaphore(ch)
        tok = self.emit(eng, lambda h: h.dma_start(out=out, in_=in_, **kw),
                        inc=ch, waits=waits, amt=16)
        return (ch, tok)

    def dma_all_done(self):
        return [(s, self.cnt[s]) for s in self.sem if s.startswith("ch_")]

    def wait(self, eng, waits):
        w = [(s, v) for (s, v) in waits if v and v > 0]
        if not w:
            return
        sem = self.sem

        def run(h, w=w):
            for s, v in w:
                h.wait_ge(sem[s], v)
        self.q[eng].append(run)

    ENG_SEM = {"vector": "ve", "scalar": "sc", "tensor": "mm", "gpsimd": "gp"}

    def flush(self, eng):
        return self.emit(eng, lambda h: h.drain(), inc=self.ENG_SEM[eng])

    def finalize(self):
        with self.nc.Block() as block:
            for e in self.ENGINES:
                ops = self.q[e]
                if not ops:
                    continue
                getattr(block, e)(lambda h, ops=ops: [f(h) for f in ops])
        self.stack.close()



def emit_ln_pass(kb, x_src_dram, xnT_dram, xnf_dram, ident_sb, tag, tile_cb=None):
    xin = [kb.sb(f"{tag}_xin{j}", [128, D], F32) for j in range(2)]
    ln = [kb.sb(f"{tag}_ln{j}", [128, D], F32) for j in range(2)]
    lnb = [kb.sb(f"{tag}_lnb{j}", [128, D], BF16) for j in range(2)]
    stg = [[kb.sb(f"{tag}_stg{j}{g}", [128, 512], BF16) for g in range(2)] for j in range(2)]
    xsq = kb.sb(f"{tag}_xsq", [128, D], F32)
    msum = kb.sb(f"{tag}_msum", [128, 1], F32)
    sqsum = kb.sb(f"{tag}_sqsum", [128, 1], F32)
    ms = kb.sb(f"{tag}_ms", [128, 1], F32)
    msq = kb.sb(f"{tag}_msq", [128, 1], F32)
    var = kb.sb(f"{tag}_var", [128, 1], F32)
    rstd = kb.sb(f"{tag}_rstd", [128, 1], F32)
    eps5 = kb.sb(f"{tag}_eps5", [128, 1], F32)
    pt = [kb.ps(f"{tag}_pt{j}", [128, 512], BF16) for j in range(2)]

    tok_eps = kb.emit("vector", lambda v: v.memset(eps5[:, :], EPS_LN), inc="ve")

    xin_free_ve = [0, 0]
    xin_free_sc = [0, 0]
    ln_free_st = [(f"ch_{tag}lnst{j}", 0) for j in range(2)]
    lnb_free_mm = [0, 0]
    ev_done = [0, 0]
    stg_free = [[(f"ch_{tag}stg{j}{g}", 0) for g in range(2)] for j in range(2)]
    prev_var = 0
    prev_apply = 0

    for i in range(NT):
        b = i % 2
        tld = kb.dma("sync", xin[b][:, :], x_src_dram[i * 128:(i + 1) * 128, :],
                     waits=[("ve", xin_free_ve[b]), ("sc", xin_free_sc[b])],
                     ch=f"ch_{tag}xin{b}")
        w_ld = [tld]
        t_red = kb.emit("vector", lambda v, b=b: v.reduce_sum(msum[:, :], xin[b][:, :], axis=AX.X),
                        inc="ve", waits=w_ld)
        t_sq = kb.emit("scalar", lambda s, b=b: s.activation(xsq[:, :], xin[b][:, :], AF.Square,
                                                             accum_out=sqsum[:, :]),
                       inc="sc", waits=w_ld + [("ve", prev_var)])
        xin_free_sc[b] = t_sq
        kb.emit("vector", lambda v: v.tensor_scalar(ms[:, :], msum[:, :], 1.0 / D, None, op0=ALU.mult),
                inc="ve", waits=[("ve", t_red), ("sc", t_sq)])
        t_ms = kb.cnt["ve"]
        kb.emit("vector", lambda v: v.tensor_tensor(msq[:, :], ms[:, :], ms[:, :], op=ALU.mult),
                inc="ve", waits=[("ve", t_ms)])
        t_msq = kb.cnt["ve"]
        t_var = kb.emit("vector", lambda v: v.scalar_tensor_tensor(var[:, :], sqsum[:, :], 1.0 / D,
                                                                   msq[:, :], op0=ALU.mult,
                                                                   op1=ALU.subtract),
                        inc="ve", waits=[("ve", t_msq)])
        prev_var = t_var
        kb.emit("scalar", lambda s: s.activation(rstd[:, :], var[:, :], AF.Ln, bias=eps5[:, 0:1]),
                waits=[("ve", t_var), ("ve", tok_eps), ("ve", prev_apply)])
        t_rstd = kb.emit("scalar", lambda s: s.activation(rstd[:, :], rstd[:, :], AF.Exp, scale=-0.5),
                         inc="sc")
        t_ln = kb.emit("vector", lambda v, b=b: v.tensor_scalar(ln[b][:, :], xin[b][:, :], ms[:, 0:1],
                                                                rstd[:, 0:1], op0=ALU.subtract,
                                                                op1=ALU.mult),
                       inc="ve", waits=[("sc", t_rstd), ln_free_st[b]])
        prev_apply = t_ln
        xin_free_ve[b] = t_ln
        t_lnb = kb.emit("vector", lambda v, b=b: v.tensor_copy(lnb[b][:, :], ln[b][:, :]),
                        inc="ve", waits=[("mm", lnb_free_mm[b])])
        if xnf_dram is not None:
            t_fl = kb.flush("vector")
            ln_free_st[b] = kb.dma("sync", xnf_dram[i * 128:(i + 1) * 128, :], ln[b][:, :],
                                   waits=[("ve", t_fl)], ch=f"ch_{tag}lnst{b}")
        for g in range(2):
            for k in range(4):
                kk = g * 4 + k
                kb.emit("tensor",
                        lambda t, b=b, kk=kk, k=k: t.transpose(
                            pt[b][:, k * 128:(k + 1) * 128], lnb[b][:, kk * 128:(kk + 1) * 128],
                            ident_sb[:, :]),
                        inc=("mm" if k == 3 else None),
                        waits=([("ve", t_lnb), ("ve", ev_done[b])] if k == 0 else []))
            t_tr = kb.cnt["mm"]
            if g == 1:
                lnb_free_mm[b] = t_tr
            t_ev = kb.emit("vector", lambda v, b=b, g=g: v.tensor_copy(stg[b][g][:, :], pt[b][:, :]),
                           inc="ve", waits=[("mm", t_tr), stg_free[b][g]])
            ev_done[b] = t_ev
            t_fl = kb.flush("vector")
            last = 0
            for k in range(4):
                kk = g * 4 + k
                last = kb.dma("sync", xnT_dram[kk * 128:(kk + 1) * 128, i * 128:(i + 1) * 128],
                              stg[b][g][:, k * 128:(k + 1) * 128],
                              waits=[("ve", t_fl)] if k == 0 else [], ch=f"ch_{tag}stg{b}{g}")
            stg_free[b][g] = last
        if tile_cb is not None:
            tile_cb(kb, i, ln[b], t_ln)



def phase_A(kb, io):
    wrb = kb.sb("A_wrb", [128, D], F32)
    onesr = kb.sb("A_onesr", [1, 128], F32)
    ones1 = kb.sb("A_ones1", [128, 1], F32)
    scores = io["scores_sb"]
    prodt = kb.sb("A_prod", [128, D], F32)
    pb = kb.ps("A_pb", [128, 512], F32)

    t_onesr = kb.emit("vector", lambda v: v.memset(onesr[:, :], 1.0), inc="ve")
    t_ones1 = kb.emit("vector", lambda v: v.memset(ones1[:, :], 1.0), inc="ve")
    t_wr_ld = kb.dma("sync", wrb[0:1, :], io["wr_row"][:, :], ch="ch_Awr")
    t_ev = 0
    for j in range(2):
        t_mm = kb.emit("tensor",
                       lambda t, j=j: t.matmul(pb[:, :], onesr[:, :], wrb[0:1, j * 512:(j + 1) * 512],
                                               start=True, stop=True),
                       inc="mm", waits=[("ve", t_onesr), t_wr_ld, ("ve", t_ev)])
        t_ev = kb.emit("vector", lambda v, j=j: v.tensor_copy(wrb[:, j * 512:(j + 1) * 512], pb[:, :]),
                       inc="ve", waits=[("mm", t_mm)])
    t_wrb = t_ev

    def tile_cb(kb, i, ln_tile, t_ln):
        kb.emit("vector", lambda v, ln_tile=ln_tile, i=i: v.scalar_tensor_tensor(
            prodt[:, :], ln_tile[:, :], 1.0, wrb[:, :], op0=ALU.mult, op1=ALU.mult,
            accum_out=scores[:, i:i + 1]),
            inc="ve", waits=[("ve", t_wrb)])

    emit_ln_pass(kb, io["x0"], io["xn0T_s"], io["xn0f_s"], io["ident_sb"], "A", tile_cb=tile_cb)
    t_scores = kb.cnt["ve"]

    lo = kb.sb("A_lo", [1, 1], F32)
    hi = kb.sb("A_hi", [1, 1], F32)
    mid = kb.sb("A_mid", [1, 1], F32)
    tmp1 = kb.sb("A_tmp1", [1, 1], F32)
    cge = kb.sb("A_cge", [1, 1], F32)
    ncge = kb.sb("A_ncge", [1, 1], F32)
    cnt_sb = kb.sb("A_cnt", [1, 1], F32)
    part = kb.sb("A_part", [128, 1], F32)
    cmp_t = kb.sb("A_cmp", [128, NT], F32)
    thrb = kb.sb("A_thrb", [128, 1], F32)
    sigs = kb.sb("A_sigs", [128, NT], F32)
    pb2 = kb.ps("A_pb2", [128, 512], F32)

    t2 = kb.emit("vector", lambda v: v.reduce_sum(part[:, :], scores[:, :], axis=AX.X,
                                                  apply_absolute_value=True),
                 inc="ve", waits=[("ve", t_scores)])
    t3 = kb.emit("tensor", lambda t: t.matmul(pb2[0:1, 0:1], ones1[:, :], part[:, :],
                                              start=True, stop=True),
                 inc="mm", waits=[("ve", t2), ("ve", t_ones1)])
    t4 = kb.emit("vector", lambda v: v.tensor_scalar(hi[:, :], pb2[0:1, 0:1], 1.0, None, op0=ALU.add),
                 inc="ve", waits=[("mm", t3)])
    t5 = kb.emit("vector", lambda v: v.tensor_scalar(lo[:, :], hi[:, :], -1.0, None, op0=ALU.mult),
                 inc="ve", waits=[("ve", t4)])
    t_mid = kb.emit("vector", lambda v: v.tensor_tensor(mid[:, :], lo[:, :], hi[:, :], op=ALU.add),
                    inc="ve", waits=[("ve", t5)])
    t_mid = kb.emit("vector", lambda v: v.tensor_scalar(mid[:, :], mid[:, :], 0.5, None, op0=ALU.mult),
                    inc="ve", waits=[("ve", t_mid)])

    for it in range(50):
        t_b = kb.emit("tensor", lambda t: t.matmul(pb2[:, 4:5], onesr[:, :], mid[0:1, 0:1],
                                                   start=True, stop=True),
                      inc="mm", waits=[("ve", t_mid)])
        t_c = kb.emit("vector", lambda v: v.tensor_copy(thrb[:, :], pb2[:, 4:5]),
                      inc="ve", waits=[("mm", t_b)])
        t_cmp = kb.emit("vector", lambda v: v.tensor_scalar(cmp_t[:, :], scores[:, :], thrb[:, 0:1],
                                                            None, op0=ALU.is_gt),
                        inc="ve", waits=[("ve", t_c)])
        t_red = kb.emit("vector", lambda v: v.reduce_sum(part[:, :], cmp_t[:, :], axis=AX.X),
                        inc="ve", waits=[("ve", t_cmp)])
        t_cnt = kb.emit("tensor", lambda t: t.matmul(pb2[0:1, 8:9], ones1[:, :], part[:, :],
                                                     start=True, stop=True),
                        inc="mm", waits=[("ve", t_red)])
        t_cc = kb.emit("vector", lambda v: v.tensor_copy(cnt_sb[:, :], pb2[0:1, 8:9]),
                       inc="ve", waits=[("mm", t_cnt)])
        t_ge = kb.emit("vector", lambda v: v.tensor_scalar(cge[:, :], cnt_sb[:, :], KSEL - 0.5, None,
                                                           op0=ALU.is_ge),
                       inc="ve", waits=[("ve", t_cc)])
        t_ng = kb.emit("vector", lambda v: v.tensor_scalar(ncge[:, :], cge[:, :], -1.0, 1.0,
                                                           op0=ALU.mult, op1=ALU.add),
                       inc="ve", waits=[("ve", t_ge)])
        tk = kb.emit("vector", lambda v: v.tensor_tensor(tmp1[:, :], mid[:, :], lo[:, :], op=ALU.subtract),
                     inc="ve", waits=[("ve", t_ng)])
        tk = kb.emit("vector", lambda v: v.tensor_tensor(tmp1[:, :], tmp1[:, :], cge[:, :], op=ALU.mult),
                     inc="ve", waits=[("ve", tk)])
        tk = kb.emit("vector", lambda v: v.tensor_tensor(lo[:, :], lo[:, :], tmp1[:, :], op=ALU.add),
                     inc="ve", waits=[("ve", tk)])
        tk = kb.emit("vector", lambda v: v.tensor_tensor(tmp1[:, :], mid[:, :], hi[:, :], op=ALU.subtract),
                     inc="ve", waits=[("ve", tk)])
        tk = kb.emit("vector", lambda v: v.tensor_tensor(tmp1[:, :], tmp1[:, :], ncge[:, :], op=ALU.mult),
                     inc="ve", waits=[("ve", tk)])
        tk = kb.emit("vector", lambda v: v.tensor_tensor(hi[:, :], hi[:, :], tmp1[:, :], op=ALU.add),
                     inc="ve", waits=[("ve", tk)])
        t_mid = kb.emit("vector", lambda v: v.tensor_tensor(mid[:, :], lo[:, :], hi[:, :], op=ALU.add),
                        inc="ve", waits=[("ve", tk)])
        t_mid = kb.emit("vector", lambda v: v.tensor_scalar(mid[:, :], mid[:, :], 0.5, None, op0=ALU.mult),
                        inc="ve", waits=[("ve", t_mid)])

    t_bl = kb.emit("tensor", lambda t: t.matmul(pb2[:, 12:13], onesr[:, :], lo[0:1, 0:1],
                                                start=True, stop=True),
                   inc="mm", waits=[("ve", t_mid)])
    t_cl = kb.emit("vector", lambda v: v.tensor_copy(thrb[:, :], pb2[:, 12:13]),
                   inc="ve", waits=[("mm", t_bl)])
    t_m = kb.emit("vector", lambda v: v.tensor_scalar(cmp_t[:, :], scores[:, :], thrb[:, 0:1], None,
                                                      op0=ALU.is_gt),
                  inc="ve", waits=[("ve", t_cl)])
    t_sg = kb.emit("scalar", lambda s: s.activation(sigs[:, :], scores[:, :], AF.Sigmoid),
                   inc="sc", waits=[("ve", t_scores)])
    t_w = kb.emit("vector", lambda v: v.tensor_tensor(io["w_sb"][:, :], sigs[:, :], cmp_t[:, :],
                                                      op=ALU.mult),
                  inc="ve", waits=[("ve", t_m), ("sc", t_sg)])
    io["tok_w"] = t_w
    io["tok_scores"] = t_scores



def phase_B(kb, io, dbg=False):
    wqb = kb.sb("B_wqb", [128, 8 * 1024], BF16)
    wkb = kb.sb("B_wkb", [128, 8 * 1024], BF16)
    wvb = kb.sb("B_wvb", [128, 8 * 1024], BF16)
    wob = kb.sb("B_wob", [128, 8 * 1024], BF16)
    wg1b = kb.sb("B_wg1b", [128, 8 * 64], BF16)
    wg2b = kb.sb("B_wg2b", [64, 1024], BF16)
    nw8 = kb.sb("B_nw8", [128, 8], F32)
    wtmp = kb.sb("B_wtmp", [128, 1024], F32)
    eps6 = kb.sb("B_eps6", [128, 1], F32)
    umask = kb.sb("B_umask", [128, 128], F32)
    zeros128 = kb.sb("B_zeros", [128, 128], F32)

    for ct in range(NCT):
        kb.dma("gpsimd", wqb[:, ct * 1024:(ct + 1) * 1024],
               io["wq"][ct * 128:(ct + 1) * 128, :], ch="ch_Bw")
        kb.dma("gpsimd", wkb[:, ct * 1024:(ct + 1) * 1024],
               io["wk"][ct * 128:(ct + 1) * 128, :], ch="ch_Bw")
        kb.dma("gpsimd", wvb[:, ct * 1024:(ct + 1) * 1024],
               io["wv"][ct * 128:(ct + 1) * 128, :], ch="ch_Bw")
        kb.dma("gpsimd", wg1b[:, ct * 64:(ct + 1) * 64],
               io["wg1"][ct * 128:(ct + 1) * 128, :], ch="ch_Bw")
    t_wlast = kb.dma("gpsimd", wg2b[:, :], io["wg2"][:, :], ch="ch_Bw")
    t_nw = kb.dma("sync", nw8[:, :], io["nw8"][:, :], ch="ch_Bnw")
    t_um = kb.dma("sync", umask[:, :], io["umask"][:, :], ch="ch_Bum")
    t_prev = 0
    for ct in range(NCT):
        t_l = kb.dma("sync", wtmp[:, :], io["wo"][ct * 128:(ct + 1) * 128, :],
                     waits=[("ve", t_prev)], ch="ch_Bwt")
        t_prev = kb.emit("vector", lambda v, ct=ct: v.tensor_scalar(
            wob[:, ct * 1024:(ct + 1) * 1024], wtmp[:, :], nw8[:, ct:ct + 1], None, op0=ALU.mult),
            inc="ve", waits=[t_l, t_nw])
    tok_wo = t_prev
    tok_eps6 = kb.emit("vector", lambda v: v.memset(eps6[:, :], EPS_RMS), inc="ve")
    tok_z128 = kb.emit("vector", lambda v: v.memset(zeros128[:, :], 0.0), inc="ve")

    carr = kb.sb("B_carr", [128, 8], F32)
    SCb = kb.sb("B_SCb", [128, 512], BF16)
    SSb = kb.sb("B_SSb", [128, 512], BF16)
    t_init = [kb.emit("vector", lambda v: v.memset(carr[:, :], 1.0), inc="ve")]
    for bufm in (SCb, SSb):
        t_init.append(kb.emit("vector", lambda v, bufm=bufm: v.memset(bufm[:, :], 0.0), inc="ve"))
    tok_init = max(t_init)

    xg = kb.sb("B_xg", [128, 8 * 512], BF16)
    qT = kb.sb("B_qT", [128, 8 * 512], BF16)
    ek = kb.sb("B_ek", [128, 8 * 512], BF16)
    vg = kb.sb("B_vg", [128, 4 * 1024], BF16)
    sgg = kb.sb("B_sg", [128, 4 * 1024], F32)
    xg1 = kb.sb("B_xg1", [64, 512], BF16)
    cosg = kb.sb("B_cos", [128, 4096], BF16)
    sing = kb.sb("B_sin", [128, 4096], BF16)
    zg = kb.sb("B_zg", [128, 1024], F32)
    rz = kb.sb("B_rz", [128, 1024], F32)
    kc2 = [kb.sb(f"B_kc{j}", [128, 1024], BF16) for j in range(2)]
    ksn2 = [kb.sb(f"B_ks{j}", [128, 1024], BF16) for j in range(2)]
    qc2 = [kb.sb(f"B_qc{j}", [128, 1024], BF16) for j in range(2)]
    qs2 = [kb.sb(f"B_qs{j}", [128, 1024], BF16) for j in range(2)]
    tmpf = kb.sb("B_tmpf", [128, 1024], F32)
    kct2 = [kb.sb(f"B_kct{j}", [128, 1024], BF16) for j in range(2)]
    kst2 = [kb.sb(f"B_kst{j}", [128, 1024], BF16) for j in range(2)]
    atm = [kb.sb(f"B_atm{j}", [128, 128], BF16) for j in range(2)]
    gc = kb.sb("B_gc", [128, 1024], F32)
    gsq = kb.sb("B_gsq", [128, 1024], F32)
    rmsq = kb.sb("B_rmsq", [128, 16], F32)
    rstd16 = kb.sb("B_rstd16", [128, 16], F32)
    nrm = kb.sb("B_nrm", [128, 1024], BF16)
    nrmT = kb.sb("B_nrmT", [128, 1024], BF16)
    sstg = kb.sb("B_sstg", [128, 1024], F32)

    pj = [kb.ps(f"B_pj{j}", [128, 512], F32) for j in range(2)]
    ptr = [kb.ps(f"B_ptr{j}", [128, 512], BF16) for j in range(2)]
    pat = [kb.ps(f"B_pat{j}", [128, 512], F32) for j in range(2)]
    pSC = kb.ps("B_pSC", [128, 512], F32)
    pSS = kb.ps("B_pSS", [128, 512], F32)

    pj_free = [("ve", 0), ("ve", 0)]
    ptr_free = [("ve", 0), ("ve", 0)]
    pat_free = [("ve", 0), ("ve", 0)]
    spsum_free = ("ve", 0)
    gd_mm = 0
    gd_ve = 0
    gd_sc = 0
    carr_tok = [tok_init] * NCT
    scb_tok = tok_init
    sstg_free = ("ch_Bs0", 0)
    tok_sq_gc = 0
    chunk_mm_tok = [0, 0]
    nrm_tr_tok = 0

    kb.wait("tensor", [t_um])

    def dump(name, src_ap, waits):
        if dbg and name in io:
            kb.dma("sync", io[name][:, :], src_ap, waits=waits)

    for g in range(8):
        t_xg = [kb.dma("sync", xg[:, ct * 512:(ct + 1) * 512],
                       io["xn0T_s"][ct * 128:(ct + 1) * 128, g * 512:(g + 1) * 512],
                       waits=([("mm", gd_mm)] if ct == 0 else []), ch="ch_Bxg")
                for ct in range(NCT)]
        t_cos = kb.dma("gpsimd", cosg[:, :], io["cosT"][:, g * 4096:(g + 1) * 4096],
                       waits=[("ve", gd_ve)], ch="ch_Bcos")
        t_sin = kb.dma("gpsimd", sing[:, :], io["sinT"][:, g * 4096:(g + 1) * 4096],
                       ch="ch_Bsin")
        t_xgl = t_xg[-1]

        qT_tok = [0] * NCT
        ek_tok = [0] * NCT
        for dt in range(NCT):
            bank = dt % 2
            for j in range(NCT):
                kb.emit("tensor", lambda t, j=j, dt=dt, bank=bank: t.matmul(
                    pj[bank][:, :],
                    wqb[:, j * 1024 + dt * 128: j * 1024 + (dt + 1) * 128],
                    xg[:, j * 512:(j + 1) * 512],
                    start=(j == 0), stop=(j == NCT - 1)),
                    inc=("mm" if j == NCT - 1 else None),
                    waits=([t_xgl, t_wlast, pj_free[bank],
                            ("ve", gd_ve), ("sc", gd_sc)] if j == 0 else []))
            tok = kb.cnt["mm"]
            qT_tok[dt] = kb.emit("scalar", lambda s, dt=dt, bank=bank: s.activation(
                qT[:, dt * 512:(dt + 1) * 512], pj[bank][:, :], AF.Silu),
                inc="sc", waits=[("mm", tok)])
            pj_free[bank] = ("sc", qT_tok[dt])
        for dt in range(NCT):
            bank = dt % 2
            for j in range(NCT):
                kb.emit("tensor", lambda t, j=j, dt=dt, bank=bank: t.matmul(
                    pj[bank][:, :],
                    wkb[:, j * 1024 + dt * 128: j * 1024 + (dt + 1) * 128],
                    xg[:, j * 512:(j + 1) * 512],
                    start=(j == 0), stop=(j == NCT - 1)),
                    inc=("mm" if j == NCT - 1 else None),
                    waits=([pj_free[bank]] if j == 0 else []))
            tok = kb.cnt["mm"]
            ek_tok[dt] = kb.emit("scalar", lambda s, dt=dt, bank=bank: s.activation(
                ek[:, dt * 512:(dt + 1) * 512], pj[bank][:, :], AF.Exp),
                inc="sc", waits=[("mm", tok)])
            pj_free[bank] = ("sc", ek_tok[dt])
        vg_tok = 0
        for m in range(4):
            for dch in range(2):
                bank = (m * 2 + dch) % 2
                for j in range(NCT):
                    kb.emit("tensor", lambda t, j=j, m=m, dch=dch, bank=bank: t.matmul(
                        pj[bank][:, :],
                        xg[:, j * 512 + m * 128: j * 512 + (m + 1) * 128],
                        wvb[:, j * 1024 + dch * 512: j * 1024 + (dch + 1) * 512],
                        start=(j == 0), stop=(j == NCT - 1)),
                        inc=("mm" if j == NCT - 1 else None),
                        waits=([pj_free[bank]] if j == 0 else []))
                tok = kb.cnt["mm"]
                vg_tok = kb.emit("vector", lambda v, m=m, dch=dch, bank=bank: v.tensor_copy(
                    vg[:, m * 1024 + dch * 512: m * 1024 + (dch + 1) * 512], pj[bank][:, :]),
                    inc="ve", waits=[("mm", tok)])
                pj_free[bank] = ("ve", vg_tok)
        if g == 0:
            dump("d_ek", ek[:, :], [("sc", ek_tok[-1])])
            dump("d_qT", qT[:, :], [("sc", qT_tok[-1])])
            dump("d_vg", vg[:, :], [("ve", vg_tok)])
        bank = 0
        for j in range(NCT):
            kb.emit("tensor", lambda t, j=j: t.matmul(
                pj[0][0:64, :], wg1b[:, j * 64:(j + 1) * 64],
                xg[:, j * 512:(j + 1) * 512],
                start=(j == 0), stop=(j == NCT - 1)),
                inc=("mm" if j == NCT - 1 else None),
                waits=([pj_free[0]] if j == 0 else []))
        tok = kb.cnt["mm"]
        t_xg1 = kb.emit("vector", lambda v: v.tensor_copy(xg1[:, :], pj[0][0:64, :]),
                        inc="ve", waits=[("mm", tok)])
        pj_free[0] = ("ve", t_xg1)
        sg_tok = 0
        for m in range(4):
            for dch in range(2):
                bank = (m * 2 + dch) % 2
                tok = kb.emit("tensor", lambda t, m=m, dch=dch, bank=bank: t.matmul(
                    pj[bank][:, :], xg1[0:64, m * 128:(m + 1) * 128],
                    wg2b[0:64, dch * 512:(dch + 1) * 512], start=True, stop=True),
                    inc="mm", waits=[pj_free[bank], ("ve", t_xg1)])
                sg_tok = kb.emit("scalar", lambda s, m=m, dch=dch, bank=bank: s.activation(
                    sgg[:, m * 1024 + dch * 512: m * 1024 + (dch + 1) * 512], pj[bank][:, :],
                    AF.Sigmoid), inc="sc", waits=[("mm", tok)])
                pj_free[bank] = ("sc", sg_tok)

        if g == 0:
            dump("d_sg", sgg[:, :], [("sc", sg_tok)])
        for j4 in range(4):
            c = g * 4 + j4
            cpar = c % 2
            kc = kc2[cpar]
            ksn = ksn2[cpar]
            qc = qc2[cpar]
            qs = qs2[cpar]
            kct = kct2[cpar]
            kst = kst2[cpar]
            rz_tok = [0] * NCT
            for ct in range(NCT):
                t_s = kb.emit("vector", lambda v, ct=ct, j4=j4: v.tensor_tensor_scan(
                    zg[:, ct * 128:(ct + 1) * 128],
                    ek[:, ct * 512 + j4 * 128: ct * 512 + (j4 + 1) * 128],
                    zeros128[:, :], carr[:, ct:ct + 1], op0=ALU.add, op1=ALU.add),
                    inc="ve", waits=[("sc", ek_tok[ct]), ("ve", carr_tok[ct]), ("ve", tok_z128)])
                carr_tok[ct] = kb.emit("vector", lambda v, ct=ct: v.tensor_copy(
                    carr[:, ct:ct + 1], zg[:, ct * 128 + 127: ct * 128 + 128]),
                    inc="ve", waits=[("ve", t_s)])
                rz_tok[ct] = kb.emit("vector", lambda v, ct=ct: v.reciprocal(
                    rz[:, ct * 128:(ct + 1) * 128], zg[:, ct * 128:(ct + 1) * 128]),
                    inc="ve", waits=[("ve", t_s)])
            kq_tok = 0
            for ct in range(NCT):
                co = j4 * 1024 + ct * 128
                so = ct * 512 + j4 * 128
                w0 = [t_cos, t_sin, ("ve", rz_tok[ct]), ("sc", qT_tok[ct]),
                      ("mm", chunk_mm_tok[cpar])]
                kb.emit("vector", lambda v, ct=ct, co=co, so=so, kc=kc: v.tensor_tensor(
                    kc[:, ct * 128:(ct + 1) * 128], ek[:, so:so + 128], cosg[:, co:co + 128],
                    op=ALU.mult), inc="ve", waits=w0)
                kb.emit("vector", lambda v, ct=ct, co=co, so=so, ksn=ksn: v.tensor_tensor(
                    ksn[:, ct * 128:(ct + 1) * 128], ek[:, so:so + 128], sing[:, co:co + 128],
                    op=ALU.mult), inc="ve")
                kb.emit("vector", lambda v, co=co, so=so: v.tensor_tensor(
                    tmpf[:, 0:128], qT[:, so:so + 128], cosg[:, co:co + 128], op=ALU.mult),
                    inc="ve")
                kb.emit("vector", lambda v, ct=ct, qc=qc: v.tensor_tensor(
                    qc[:, ct * 128:(ct + 1) * 128], tmpf[:, 0:128], rz[:, ct * 128:(ct + 1) * 128],
                    op=ALU.mult), inc="ve")
                kb.emit("vector", lambda v, co=co, so=so: v.tensor_tensor(
                    tmpf[:, 128:256], qT[:, so:so + 128], sing[:, co:co + 128], op=ALU.mult),
                    inc="ve")
                kq_tok = kb.emit("vector", lambda v, ct=ct, qs=qs: v.tensor_tensor(
                    qs[:, ct * 128:(ct + 1) * 128], tmpf[:, 128:256], rz[:, ct * 128:(ct + 1) * 128],
                    op=ALU.mult), inc="ve")
            if dbg and g == 0 and j4 == 0:
                dump("d_zg", zg[:, :], [("ve", rz_tok[-1])])
                dump("d_rz", rz[:, :], [("ve", rz_tok[-1])])
                dump("d_kc", kc[:, :], [("ve", kq_tok)])
                dump("d_qc", qc[:, :], [("ve", kq_tok)])
            kt_tok = 0
            for (src, dst) in ((kc, kct), (ksn, kst)):
                for half in range(2):
                    for q4 in range(4):
                        ct = half * 4 + q4
                        kb.emit("tensor", lambda t, src=src, ct=ct, q4=q4, half=half: t.transpose(
                            ptr[half][:, q4 * 128:(q4 + 1) * 128],
                            src[:, ct * 128:(ct + 1) * 128], io["ident_sb"][:, :]),
                            inc=("mm" if q4 == 3 else None),
                            waits=([("ve", kq_tok), ptr_free[half]] if q4 == 0 else []))
                    t_tr = kb.cnt["mm"]
                    tk = kb.emit("vector", lambda v, dst=dst, half=half: v.tensor_copy(
                        dst[:, half * 512:(half + 1) * 512], ptr[half][:, :]),
                        inc="ve", waits=[("mm", t_tr)])
                    ptr_free[half] = ("ve", tk)
                    kt_tok = tk
            gc_toks = []
            new_scb = []
            for h in range(H):
                par = h % 2
                dt = h // 2
                base = par * 64
                bank = h % 2
                kb.emit("tensor", lambda t, base=base, dt=dt, bank=bank, kc=kc, qc=qc: t.matmul(
                    pat[bank][:, 0:128], kc[base:base + 64, dt * 128:(dt + 1) * 128],
                    qc[base:base + 64, dt * 128:(dt + 1) * 128], start=True, stop=False),
                    waits=[("ve", kq_tok), pat_free[bank]])
                t_at = kb.emit("tensor", lambda t, base=base, dt=dt, bank=bank, ksn=ksn, qs=qs: t.matmul(
                    pat[bank][:, 0:128], ksn[base:base + 64, dt * 128:(dt + 1) * 128],
                    qs[base:base + 64, dt * 128:(dt + 1) * 128], start=False, stop=True),
                    inc="mm")
                t_atm = kb.emit("vector", lambda v, bank=bank: v.tensor_tensor(
                    atm[bank][:, :], pat[bank][:, 0:128], umask[:, :], op=ALU.mult),
                    inc="ve", waits=[("mm", t_at)])
                kb.emit("tensor", lambda t, bank=bank, j4=j4, h=h: t.matmul(
                    pat[bank][:, 128:192], atm[bank][:, :],
                    vg[:, j4 * 1024 + h * 64: j4 * 1024 + (h + 1) * 64],
                    start=True, stop=False),
                    waits=[("ve", t_atm), ("ve", vg_tok), ("ve", scb_tok)])
                kb.emit("tensor", lambda t, bank=bank, base=base, dt=dt, qc=qc: t.matmul(
                    pat[bank][:, 128:192], qc[base:base + 64, dt * 128:(dt + 1) * 128],
                    SCb[base:base + 64, dt * 64:(dt + 1) * 64], start=False, stop=False))
                t_o = kb.emit("tensor", lambda t, bank=bank, base=base, dt=dt, qs=qs: t.matmul(
                    pat[bank][:, 128:192], qs[base:base + 64, dt * 128:(dt + 1) * 128],
                    SSb[base:base + 64, dt * 64:(dt + 1) * 64], start=False, stop=True),
                    inc="mm")
                t_gc = kb.emit("vector", lambda v, bank=bank, j4=j4, h=h: v.tensor_tensor(
                    gc[:, h * 64:(h + 1) * 64], pat[bank][:, 128:192],
                    sgg[:, j4 * 1024 + h * 64: j4 * 1024 + (h + 1) * 64], op=ALU.mult),
                    inc="ve", waits=[("mm", t_o), ("sc", sg_tok), ("sc", tok_sq_gc)])
                gc_toks.append(t_gc)
                pat_free[bank] = ("ve", t_gc)
                first = (c == 0 and h == 0)
                kb.emit("tensor", lambda t, base=base, h=h, j4=j4, kct=kct, first=first: t.matmul(
                    pSC[base:base + 64, (h // 2) * 64:(h // 2 + 1) * 64],
                    kct[:, h * 64:(h + 1) * 64],
                    vg[:, j4 * 1024 + h * 64: j4 * 1024 + (h + 1) * 64],
                    start=first, stop=False),
                    waits=[("ve", kt_tok)] + ([spsum_free] if h == 0 else []))
                t_st = kb.emit("tensor", lambda t, base=base, h=h, j4=j4, kst=kst, first=first: t.matmul(
                    pSS[base:base + 64, (h // 2) * 64:(h // 2 + 1) * 64],
                    kst[:, h * 64:(h + 1) * 64],
                    vg[:, j4 * 1024 + h * 64: j4 * 1024 + (h + 1) * 64],
                    start=first, stop=False),
                    inc="mm")
                new_scb.append(t_st)
            t_stl = kb.cnt["mm"]
            t_sc1 = kb.emit("vector", lambda v: v.tensor_copy(SCb[:, :], pSC[:, :]),
                            inc="ve", waits=[("mm", t_stl)])
            t_sc2 = kb.emit("vector", lambda v: v.tensor_copy(SSb[:, :], pSS[:, :]),
                            inc="ve")
            spsum_free = ("ve", t_sc2)
            scb_tok = t_sc2
            t_gc_last = max(gc_toks)
            if dbg and g == 0 and j4 == 0:
                dump("d_kct", kct[:, :], [("ve", kt_tok)])
                dump("d_gc", gc[:, :], [("ve", t_gc_last)])
                dump("d_SCb", SCb[:, :], [("ve", scb_tok)])
            tok_sq_gc = kb.emit("scalar", lambda s: s.activation(gsq[:, :], gc[:, :], AF.Square),
                                inc="sc", waits=[("ve", t_gc_last)])
            red_toks = []
            for h in range(H):
                red_toks.append(kb.emit("vector", lambda v, h=h: v.reduce_sum(
                    rmsq[:, h:h + 1], gsq[:, h * 64:(h + 1) * 64], axis=AX.X),
                    inc="ve", waits=([("sc", tok_sq_gc)] if h == 0 else [])))
            kb.emit("scalar", lambda s: s.activation(rstd16[:, :], rmsq[:, :], AF.Ln,
                                                     bias=eps6[:, 0:1], scale=1.0 / HD),
                    waits=[("ve", red_toks[-1]), ("ve", tok_eps6)])
            t_r16 = kb.emit("scalar", lambda s: s.activation(rstd16[:, :], rstd16[:, :], AF.Exp,
                                                             scale=-0.5), inc="sc")
            nrm_tok = 0
            for h in range(H):
                nrm_tok = kb.emit("vector", lambda v, h=h: v.tensor_scalar(
                    nrm[:, h * 64:(h + 1) * 64], gc[:, h * 64:(h + 1) * 64],
                    rstd16[:, h:h + 1], None, op0=ALU.mult),
                    inc="ve", waits=([("sc", t_r16), ("mm", nrm_tr_tok)] if h == 0 else []))
            nrmT_tok = 0
            for half in range(2):
                for q4 in range(4):
                    ct = half * 4 + q4
                    kb.emit("tensor", lambda t, ct=ct, q4=q4, half=half: t.transpose(
                        ptr[half][:, q4 * 128:(q4 + 1) * 128], nrm[:, ct * 128:(ct + 1) * 128],
                        io["ident_sb"][:, :]),
                        inc=("mm" if q4 == 3 else None),
                        waits=([("ve", nrm_tok), ptr_free[half]] if q4 == 0 else []))
                t_tr = kb.cnt["mm"]
                nrmT_tok = kb.emit("vector", lambda v, half=half: v.tensor_copy(
                    nrmT[:, half * 512:(half + 1) * 512], ptr[half][:, :]),
                    inc="ve", waits=[("mm", t_tr)])
                ptr_free[half] = ("ve", nrmT_tok)
            nrm_tr_tok = kb.cnt["mm"]
            s_ev = 0
            for dch in range(2):
                bank = dch % 2
                for j in range(NCT):
                    kb.emit("tensor", lambda t, j=j, dch=dch, bank=bank: t.matmul(
                        pj[bank][:, :], nrmT[:, j * 128:(j + 1) * 128],
                        wob[:, j * 1024 + dch * 512: j * 1024 + (dch + 1) * 512],
                        start=(j == 0), stop=(j == NCT - 1)),
                        inc=("mm" if j == NCT - 1 else None),
                        waits=([("ve", nrmT_tok), pj_free[bank], ("ve", tok_wo),
                                sstg_free] if j == 0 else []))
                tok = kb.cnt["mm"]
                s_ev = kb.emit("vector", lambda v, dch=dch, bank=bank: v.tensor_copy(
                    sstg[:, dch * 512:(dch + 1) * 512], pj[bank][:, :]),
                    inc="ve", waits=[("mm", tok)])
                pj_free[bank] = ("ve", s_ev)
            t_fl = kb.flush("vector")
            sstg_free = kb.dma("sync", io["s0_s"][c * 128:(c + 1) * 128, :], sstg[:, :],
                               waits=[("ve", t_fl)], ch="ch_Bs0")
            chunk_mm_tok[cpar] = kb.cnt["mm"]
        gd_mm = kb.cnt["mm"]
        gd_ve = kb.cnt["ve"]
        gd_sc = kb.cnt["sc"]
    io["tok_s0_st"] = ("ch_Bs0", kb.cnt["ch_Bs0"])




def phase_C(kb, io):
    xgc = kb.sb("C_xg", [128, 8 * 512], BF16)
    w1b = kb.sb("C_w1b", [128, 8 * 2048], BF16)
    w2b = kb.sb("C_w2b", [128, 16 * 1024], BF16)
    hb = kb.sb("C_hb", [128, 16 * 512], BF16)
    pre_t = [kb.sb(f"C_pre{j}", [128, 1024], F32) for j in range(2)]
    sq_t = [kb.sb(f"C_sq{j}", [128, 1024], F32) for j in range(2)]
    p_t = [kb.sb(f"C_p{j}", [128, 1024], F32) for j in range(2)]
    z_t = [kb.sb(f"C_zz{j}", [128, 1024], F32) for j in range(2)]
    sg_t = [kb.sb(f"C_sgm{j}", [128, 1024], F32) for j in range(2)]
    b1r = kb.sb("C_b1r", [128, 32], F32)
    b2b = kb.sb("C_b2b", [128, 1024], F32)
    onesr = kb.sb("C_onesr", [1, 128], F32)
    b2row = kb.sb("C_b2row", [1, 1024], F32)
    of_t = [kb.sb(f"C_of{j}", [128, 1024], F32) for j in range(2)]
    xn0_t = [kb.sb(f"C_xn0{j}", [128, 1024], F32) for j in range(2)]
    x0_t = [kb.sb(f"C_x0{j}", [128, 1024], F32) for j in range(2)]
    s0_t = [kb.sb(f"C_s0{j}", [128, 1024], F32) for j in range(2)]
    ya = kb.sb("C_ya", [128, 1024], F32)
    yb = kb.sb("C_yb", [128, 1024], F32)
    yg = kb.sb("C_yg", [128, 1024], F32)
    ystg = [kb.sb(f"C_ystg{j}", [128, 1024], F32) for j in range(2)]

    ph = [kb.ps(f"C_ph{j}", [128, 1024], F32) for j in range(2)]
    po = [kb.ps(f"C_po{j}", [128, 512], F32) for j in range(2)]

    t_b1 = kb.dma("sync", b1r[:, :], io["b1r"][:, :])
    t_b2r = kb.dma("sync", b2row[:, :], io["b2row"][:, :])
    t_or = kb.emit("vector", lambda v: v.memset(onesr[:, :], 1.0), inc="ve")
    t_ev = 0
    for j in range(2):
        t_mm = kb.emit("tensor", lambda t, j=j: t.matmul(
            po[0][:, :], onesr[:, :], b2row[0:1, j * 512:(j + 1) * 512], start=True, stop=True),
            inc="mm", waits=[("ve", t_or), t_b2r, ("ve", t_ev)])
        t_ev = kb.emit("vector", lambda v, j=j: v.tensor_copy(b2b[:, j * 512:(j + 1) * 512],
                                                              po[0][:, :]),
                       inc="ve", waits=[("mm", t_mm)])
    tok_b2b = t_ev

    ph_free = [("ve", 0), ("ve", 0)]
    po_free = [("ve", tok_b2b), ("ve", 0)]
    xg_free_mm = 0
    slab_free_mm = 0
    buf_free = {"of": [("ve", 0), ("ve", 0)], "xn0": [("ve", 0), ("ve", 0)],
                "x0": [("ve", 0), ("ve", 0)], "s0": [("ve", 0), ("ve", 0)]}
    ystg_free = [("ch_Cy0", 0), ("ch_Cy1", 0)]
    epi_prev_sc = 0

    gel_prev = [dict(h=0, p=0, sg=0) for _ in range(2)]
    for fh in range(2):
        t_w1 = [kb.dma("gpsimd", w1b[:, ct * 2048:(ct + 1) * 2048],
                       io["W1"][ct * 128:(ct + 1) * 128, fh * 2048:(fh + 1) * 2048],
                       waits=([("mm", slab_free_mm)] if ct == 0 else []), ch="ch_Cw1")
                for ct in range(NCT)]
        t_w2 = [kb.dma("gpsimd", w2b[:, ft * 1024:(ft + 1) * 1024],
                       io["W2"][fh * 2048 + ft * 128: fh * 2048 + (ft + 1) * 128, :],
                       ch="ch_Cw2")
                for ft in range(16)]
        t_wl = t_w2[-1]
        for tc in range(8):
            t_xg = [kb.dma("sync", xgc[:, ct * 512:(ct + 1) * 512],
                           io["xn0T_s"][ct * 128:(ct + 1) * 128, tc * 512:(tc + 1) * 512],
                           waits=([("mm", xg_free_mm)] if ct == 0 else []), ch="ch_Cxg")
                    for ct in range(NCT)]
            t_xgl = t_xg[-1]
            h_toks = []
            for fp in range(8):
                bank = fp % 2
                par = fp % 2
                for half in range(2):
                    ft = fp * 2 + half
                    for j in range(NCT):
                        kb.emit("tensor", lambda t, j=j, ft=ft, bank=bank, half=half: t.matmul(
                            ph[bank][:, half * 512:(half + 1) * 512],
                            w1b[:, j * 2048 + ft * 128: j * 2048 + (ft + 1) * 128],
                            xgc[:, j * 512:(j + 1) * 512],
                            start=(j == 0), stop=(j == NCT - 1)),
                            inc=("mm" if j == NCT - 1 else None),
                            waits=([t_xgl, t_w1[-1] if fp == 0 else ("mm", 0),
                                    ph_free[bank]] if j == 0 and half == 0 else []))
                tok = kb.cnt["mm"]
                fc0 = fh * 16 + fp * 2
                gp = gel_prev[par]
                kb.emit("scalar", lambda s, bank=bank, fc0=fc0, par=par: s.activation(
                    pre_t[par][:, 0:512], ph[bank][:, 0:512], AF.Identity,
                    bias=b1r[:, fc0:fc0 + 1]),
                    waits=[("mm", tok), t_b1, ("ve", gp["h"])])
                t_pre = kb.emit("scalar", lambda s, bank=bank, fc0=fc0, par=par: s.activation(
                    pre_t[par][:, 512:1024], ph[bank][:, 512:1024], AF.Identity,
                    bias=b1r[:, fc0 + 1:fc0 + 2]),
                    inc="sc")
                ph_free[bank] = ("sc", t_pre)
                t_sq = kb.emit("scalar", lambda s, par=par: s.activation(
                    sq_t[par][:, :], pre_t[par][:, :], AF.Square),
                    inc="sc", waits=[("ve", gp["p"])])
                t_p = kb.emit("vector", lambda v, par=par: v.tensor_scalar(
                    p_t[par][:, :], sq_t[par][:, :], 2 * C_GELU * 0.044715, 2 * C_GELU,
                    op0=ALU.mult, op1=ALU.add), inc="ve", waits=[("sc", t_sq)])
                t_z = kb.emit("vector", lambda v, par=par: v.tensor_tensor(
                    z_t[par][:, :], p_t[par][:, :], pre_t[par][:, :], op=ALU.mult), inc="ve",
                    waits=[("ve", t_p), ("sc", gp["sg"])])
                t_sg = kb.emit("scalar", lambda s, par=par: s.activation(
                    sg_t[par][:, :], z_t[par][:, :], AF.Sigmoid),
                    inc="sc", waits=[("ve", t_z), ("ve", gp["h"])])
                t_h = kb.emit("vector", lambda v, fp=fp, par=par: v.tensor_tensor(
                    hb[:, fp * 1024:(fp + 1) * 1024], sg_t[par][:, :], pre_t[par][:, :],
                    op=ALU.mult),
                    inc="ve", waits=[("sc", t_sg)])
                gel_prev[par] = dict(h=t_h, p=t_p, sg=t_sg)
                h_toks.append(t_h)
            t_hl = h_toks[-1]
            for m in range(4):
                i = tc * 4 + m
                bb = m % 2
                t_ge = 0
                for dc in range(2):
                    bank = dc % 2
                    for ft in range(16):
                        kb.emit("tensor", lambda t, ft=ft, m=m, dc=dc, bank=bank: t.matmul(
                            po[bank][:, :],
                            hb[:, ft * 512 + m * 128: ft * 512 + (m + 1) * 128],
                            w2b[:, ft * 1024 + dc * 512: ft * 1024 + (dc + 1) * 512],
                            start=(ft == 0), stop=(ft == 15)),
                            inc=("mm" if ft == 15 else None),
                            waits=([("ve", t_hl), t_wl, po_free[bank]] if ft == 0 else []))
                    tok = kb.cnt["mm"]
                    if fh == 0:
                        tk = kb.emit("vector", lambda v, bb=bb, dc=dc, bank=bank: v.tensor_copy(
                            of_t[bb][:, dc * 512:(dc + 1) * 512], po[bank][:, :]),
                            inc="ve", waits=[("mm", tok), buf_free["of"][bb]])
                        po_free[bank] = ("ve", tk)
                        if dc == 1:
                            t_fl = kb.flush("vector")
                            kb.dma("sync", io["ofh_s"][i * 128:(i + 1) * 128, :],
                                   of_t[bb][:, :], waits=[("ve", t_fl)], ch="ch_Cofs")
                            buf_free["of"][bb] = ("ve", tk)
                    else:
                        if dc == 0:
                            t_of = kb.dma("sync", of_t[bb][:, :],
                                          io["ofh_s"][i * 128:(i + 1) * 128, :],
                                          waits=[buf_free["of"][bb],
                                                 ("ch_Cofs", kb.cnt.get("ch_Cofs", 0))],
                                          ch=f"ch_Cof{bb}")
                        tk = kb.emit("vector", lambda v, bb=bb, dc=dc, bank=bank: v.tensor_tensor(
                            ya[:, dc * 512:(dc + 1) * 512], of_t[bb][:, dc * 512:(dc + 1) * 512],
                            po[bank][:, :], op=ALU.add),
                            inc="ve", waits=[("mm", tok), t_of])
                        po_free[bank] = ("ve", tk)
                if fh == 1:
                    t_ya = tk
                    buf_free["of"][bb] = ("ve", t_ya)
                    if "ypre_s" in io:
                        t_fl = kb.flush("vector")
                        kb.dma("sync", io["ypre_s"][i * 128:(i + 1) * 128, :], ya[:, :],
                               waits=[("ve", t_fl)], ch="ch_dump")
                    t_xn = kb.dma("sync", xn0_t[bb][:, :], io["xn0f_s"][i * 128:(i + 1) * 128, :],
                                  waits=[buf_free["xn0"][bb]], ch=f"ch_Cxn{bb}")
                    t_x0 = kb.dma("sync", x0_t[bb][:, :], io["x0"][i * 128:(i + 1) * 128, :],
                                  waits=[buf_free["x0"][bb]], ch=f"ch_Cx0{bb}")
                    t_s0 = kb.dma("sync", s0_t[bb][:, :], io["s0_s"][i * 128:(i + 1) * 128, :],
                                  waits=[buf_free["s0"][bb]], ch=f"ch_Cs0{bb}")
                    t1 = kb.emit("vector", lambda v: v.tensor_tensor(
                        yb[:, :], ya[:, :], b2b[:, :], op=ALU.add),
                        inc="ve", waits=[("ve", t_ya), ("ve", tok_b2b), ("sc", epi_prev_sc)])
                    t2 = kb.emit("vector", lambda v, i=i: v.tensor_scalar(
                        yg[:, :], yb[:, :], io["w_sb"][:, i:i + 1], None, op0=ALU.mult),
                        inc="ve", waits=[("ve", io["tok_w"])])
                    t3 = kb.emit("vector", lambda v, bb=bb: v.tensor_tensor(
                        yb[:, :], yg[:, :], xn0_t[bb][:, :], op=ALU.add),
                        inc="ve", waits=[t_xn])
                    buf_free["xn0"][bb] = ("ve", t3)
                    t4 = kb.emit("scalar", lambda s: s.activation(yg[:, :], yb[:, :], AF.Silu),
                                 inc="sc", waits=[("ve", t3)])
                    epi_prev_sc = t4
                    t5 = kb.emit("vector", lambda v, bb=bb: v.tensor_tensor(
                        yb[:, :], s0_t[bb][:, :], yg[:, :], op=ALU.mult),
                        inc="ve", waits=[("sc", t4), t_s0])
                    buf_free["s0"][bb] = ("ve", t5)
                    t6 = kb.emit("vector", lambda v, bb=bb, m=m: v.tensor_tensor(
                        ystg[m % 2][:, :], x0_t[bb][:, :], yb[:, :], op=ALU.add),
                        inc="ve", waits=[("ve", t5), t_x0, ystg_free[m % 2]])
                    buf_free["x0"][bb] = ("ve", t6)
                    t_fl = kb.flush("vector")
                    ystg_free[m % 2] = kb.dma("sync", io["x1_s"][i * 128:(i + 1) * 128, :],
                                              ystg[m % 2][:, :], waits=[("ve", t_fl)],
                                              ch=f"ch_Cy{m % 2}")
            xg_free_mm = kb.cnt["mm"]
        slab_free_mm = kb.cnt["mm"]
    io["tok_x1_st"] = ("ch_Dnxin", 0)



def phase_D(kb, io):
    gwb = kb.sb("D_gwb", [128, 8 * 1024], F32)
    gbb = kb.sb("D_gbb", [128, 8], F32)
    onesr = kb.sb("D_onesr", [1, 128], F32)
    growt = kb.sb("D_grow", [1, 8 * 1024], F32)
    gbrow = kb.sb("D_gbrow", [1, 8], F32)
    prodt = kb.sb("D_prod", [128, D], F32)
    lg8 = kb.sb("D_lg8", [128, 8], F32)
    lgb = kb.sb("D_lgb", [128, 8], F32)
    srt = kb.sb("D_srt", [128, 8], F32)
    d12 = kb.sb("D_d12", [128, 1], F32)
    w1n = kb.sb("D_w1n", [128, 1], F32)
    w2n = kb.sb("D_w2n", [128, 1], F32)
    eq1 = kb.sb("D_eq1", [128, 8], F32)
    eq2 = kb.sb("D_eq2", [128, 8], F32)
    c1 = kb.sb("D_c1", [128, 8], F32)
    pd = kb.ps("D_pd", [128, 512], F32)

    t_gw = kb.dma("sync", growt[:, :], io["gwT"][:, :], ch="ch_Dgw")
    t_gb = kb.dma("sync", gbrow[:, :], io["gbrow"][:, :], ch="ch_Dgb")
    t_or = kb.emit("vector", lambda v: v.memset(onesr[:, :], 1.0), inc="ve")
    t_ev = 0
    for e in range(NE):
        for j in range(2):
            t_mm = kb.emit("tensor", lambda t, e=e, j=j: t.matmul(
                pd[:, :], onesr[:, :],
                growt[0:1, e * 1024 + j * 512: e * 1024 + (j + 1) * 512],
                start=True, stop=True),
                inc="mm", waits=[("ve", t_or), t_gw, ("ve", t_ev)])
            t_ev = kb.emit("vector", lambda v, e=e, j=j: v.tensor_copy(
                gwb[:, e * 1024 + j * 512: e * 1024 + (j + 1) * 512], pd[:, :]),
                inc="ve", waits=[("mm", t_mm)])
    t_mm = kb.emit("tensor", lambda t: t.matmul(pd[:, 0:8], onesr[:, :], gbrow[0:1, :],
                                                start=True, stop=True),
                   inc="mm", waits=[t_gb, ("ve", t_ev)])
    t_gbb = kb.emit("vector", lambda v: v.tensor_copy(gbb[:, :], pd[:, 0:8]),
                    inc="ve", waits=[("mm", t_mm)])

    def tile_cb(kb, i, ln_tile, t_ln):
        red_toks = []
        for e in range(NE):
            red_toks.append(kb.emit("vector", lambda v, ln_tile=ln_tile, e=e: v.scalar_tensor_tensor(
                prodt[:, :], ln_tile[:, :], 1.0, gwb[:, e * 1024:(e + 1) * 1024],
                op0=ALU.mult, op1=ALU.mult, accum_out=lg8[:, e:e + 1]),
                inc="ve", waits=([("ve", t_gbb)] if e == 0 else [])))
        tk = kb.emit("vector", lambda v: v.tensor_tensor(lgb[:, :], lg8[:, :], gbb[:, :],
                                                         op=ALU.add),
                     inc="ve", waits=[("ve", red_toks[-1])])
        tk = kb.emit("vector", lambda v: v.max(srt[:, :], lgb[:, :]), inc="ve",
                     waits=[("ve", tk)])
        tk = kb.emit("vector", lambda v: v.tensor_tensor(d12[:, :], srt[:, 0:1], srt[:, 1:2],
                                                         op=ALU.subtract),
                     inc="ve", waits=[("ve", tk)])
        t_w1n = kb.emit("scalar", lambda s: s.activation(w1n[:, :], d12[:, :], AF.Sigmoid),
                        inc="sc", waits=[("ve", tk)])
        t_w2n = kb.emit("vector", lambda v: v.tensor_scalar(w2n[:, :], w1n[:, :], -1.0, 1.0,
                                                            op0=ALU.mult, op1=ALU.add),
                        inc="ve", waits=[("sc", t_w1n)])
        t_e1 = kb.emit("vector", lambda v: v.tensor_scalar(eq1[:, :], lgb[:, :], srt[:, 0:1],
                                                           None, op0=ALU.is_equal),
                       inc="ve", waits=[("ve", t_w2n)])
        t_e2 = kb.emit("vector", lambda v: v.tensor_scalar(eq2[:, :], lgb[:, :], srt[:, 1:2],
                                                           None, op0=ALU.is_equal),
                       inc="ve", waits=[("ve", t_e1)])
        t_c1 = kb.emit("vector", lambda v: v.tensor_scalar(c1[:, :], eq1[:, :], w1n[:, 0:1],
                                                           None, op0=ALU.mult),
                       inc="ve", waits=[("ve", t_e2)])
        t_c2 = kb.emit("vector", lambda v, i=i: v.tensor_scalar(
            eq1[:, :], eq2[:, :], w2n[:, 0:1], None, op0=ALU.mult),
            inc="ve", waits=[("ve", t_c1)])
        kb.emit("vector", lambda v, i=i: v.tensor_tensor(
            io["comb_sb"][:, i * 8:(i + 1) * 8], c1[:, :], eq1[:, :], op=ALU.add),
            inc="ve", waits=[("ve", t_c2)])

    emit_ln_pass(kb, io["x1_s"], io["xn1T_s"], None, io["ident_sb"], "Dn", tile_cb=tile_cb)
    io["tok_comb"] = kb.cnt["ve"]
    if "comb_dump" in io:
        t_fl = kb.flush("vector")
        kb.dma("sync", io["comb_dump"][:, :], io["comb_sb"][:, :],
               waits=[("ve", t_fl)], ch="ch_dump")
    io["tok_xn1_st"] = ("ve", kb.cnt["ve"])




def _e1(kb, io):
    wqb = kb.sb("E_wqb", [128, 8 * 1024], BF16)
    wkb = kb.sb("E_wkb", [128, 8 * 1024], BF16)
    wvb = kb.sb("E_wvb", [128, 8 * 1024], BF16)
    xg = kb.sb("E_xg", [128, 8 * 512], BF16)
    stg = [kb.sb(f"E_stg{j}", [128, 512], BF16) for j in range(2)]
    pj = [kb.ps(f"E_pj{j}", [128, 512], F32) for j in range(2)]

    for ct in range(NCT):
        kb.dma("gpsimd", wqb[:, ct * 1024:(ct + 1) * 1024],
               io["l1wq"][ct * 128:(ct + 1) * 128, :], ch="ch_Ew")
        kb.dma("gpsimd", wkb[:, ct * 1024:(ct + 1) * 1024],
               io["l1wk"][ct * 128:(ct + 1) * 128, :], ch="ch_Ew")
        t_wl = kb.dma("gpsimd", wvb[:, ct * 1024:(ct + 1) * 1024],
                      io["l1wv"][ct * 128:(ct + 1) * 128, :], ch="ch_Ew")

    pj_free = [("ve", 0), ("ve", 0)]
    stg_free = [("ve", 0), ("ve", 0)]
    stg_dma = [("ch_Est0", 0), ("ch_Est1", 0)]
    xg_free_mm = 0
    for g in range(8):
        t_xg = [kb.dma("sync", xg[:, ct * 512:(ct + 1) * 512],
                       io["xn1T_s"][ct * 128:(ct + 1) * 128, g * 512:(g + 1) * 512],
                       waits=([("mm", xg_free_mm)] if ct == 0 else []), ch="ch_Exg")
                for ct in range(NCT)]
        t_xgl = t_xg[-1]
        for (wb, dst, chn) in ((wqb, io["QT_s"], "q"), (wkb, io["KT_s"], "k")):
            for dt in range(NCT):
                bank = dt % 2
                sb = dt % 2
                for j in range(NCT):
                    kb.emit("tensor", lambda t, j=j, dt=dt, bank=bank, wb=wb: t.matmul(
                        pj[bank][:, :],
                        wb[:, j * 1024 + dt * 128: j * 1024 + (dt + 1) * 128],
                        xg[:, j * 512:(j + 1) * 512],
                        start=(j == 0), stop=(j == NCT - 1)),
                        inc=("mm" if j == NCT - 1 else None),
                        waits=([t_xgl, t_wl, pj_free[bank], stg_free[sb], stg_dma[sb]]
                               if j == 0 else []))
                tok = kb.cnt["mm"]
                t_ev = kb.emit("vector", lambda v, sb=sb, bank=bank: v.tensor_copy(
                    stg[sb][:, :], pj[bank][:, :]), inc="ve", waits=[("mm", tok)])
                pj_free[bank] = ("ve", t_ev)
                stg_free[sb] = ("ve", t_ev)
                t_fl = kb.flush("vector")
                stg_dma[sb] = kb.dma("sync", dst[dt * 128:(dt + 1) * 128, g * 512:(g + 1) * 512],
                                     stg[sb][:, :], waits=[("ve", t_fl)], ch=f"ch_Est{sb}")
                yield
        for m in range(4):
            for dch in range(2):
                bank = (m * 2 + dch) % 2
                sb = (m * 2 + dch) % 2
                for j in range(NCT):
                    kb.emit("tensor", lambda t, j=j, m=m, dch=dch, bank=bank: t.matmul(
                        pj[bank][:, :],
                        xg[:, j * 512 + m * 128: j * 512 + (m + 1) * 128],
                        wvb[:, j * 1024 + dch * 512: j * 1024 + (dch + 1) * 512],
                        start=(j == 0), stop=(j == NCT - 1)),
                        inc=("mm" if j == NCT - 1 else None),
                        waits=([pj_free[bank], stg_free[sb], stg_dma[sb]] if j == 0 else []))
                tok = kb.cnt["mm"]
                t_ev = kb.emit("vector", lambda v, sb=sb, bank=bank: v.tensor_copy(
                    stg[sb][:, :], pj[bank][:, :]), inc="ve", waits=[("mm", tok)])
                pj_free[bank] = ("ve", t_ev)
                stg_free[sb] = ("ve", t_ev)
                t_fl = kb.flush("vector")
                stg_dma[sb] = kb.dma("sync",
                                     io["V_s"][(g * 4 + m) * 128:(g * 4 + m + 1) * 128,
                                               dch * 512:(dch + 1) * 512],
                                     stg[sb][:, :], waits=[("ve", t_fl)], ch=f"ch_Est{sb}")
                yield
        xg_free_mm = kb.cnt["mm"]


def _e2(kb, io):
    qhp = [kb.sb(f"E_qhp{j}", [128, 4096], BF16) for j in range(2)]
    khp = [kb.sb(f"E_khp{j}", [128, 4096], BF16) for j in range(2)]
    vh = [kb.sb(f"E_vh{j}", [128, 32 * 64], BF16) for j in range(2)]
    vex = [kb.sb(f"E_vex{j}", [128, 32 * 65], BF16) for j in range(2)]
    pt2 = [kb.sb(f"E_pt{i}", [128, 1024], BF16) for i in range(2)]
    oc = [kb.sb(f"E_oc{j}", [65, 512], F32) for j in range(2)]
    rs = [kb.sb(f"E_rs{j}", [1, 512], F32) for j in range(2)]
    obf = [kb.sb(f"E_obf{j}", [64, 512], BF16) for j in range(2)]
    onesr = kb.sb("E_onesr", [1, 64], F32)
    pS = [kb.ps(f"E_pS{i}", [128, 1024], F32) for i in range(2)]
    pO = [kb.ps(f"E_pO{j}", [128, 512], F32) for j in range(2)]

    t_or = kb.emit("vector", lambda v: v.memset(onesr[:, :], 1.0), inc="ve")
    hload_free = [("mm", 0), ("mm", 0)]
    vex_free = [("mm", 0), ("mm", 0)]
    pS_free = [("sc", 0)] * 2
    pt_free = [("mm", 0)] * 2
    pO_free = [("ve", 0), ("ve", 0)]
    oc_free = [("ve", 0), ("ve", 0)]
    rs_free = [("mm", 0), ("mm", 0)]
    obf_dma = [("ch_Eo0", 0), ("ch_Eo1", 0)]
    for hp in range(H // 2):
        pb_ = hp % 2
        t_vex = [0, 0]
        t_q = kb.dma("sync", qhp[pb_][:, :], io["QT_s"][hp * 128:(hp + 1) * 128, :],
                     waits=[hload_free[pb_]], ch=f"ch_Eqh{pb_}")
        t_k = kb.dma("sync", khp[pb_][:, :], io["KT_s"][hp * 128:(hp + 1) * 128, :],
                     ch=f"ch_Ekh{pb_}")
        t_ld = [None, None]
        for j in range(2):
            h = hp * 2 + j
            t_v = [kb.dma("sync", vh[j][:, kt * 64:(kt + 1) * 64],
                          io["V_s"][kt * 128:(kt + 1) * 128, h * 64:(h + 1) * 64],
                          ch=f"ch_Evh{j}")
                   for kt in range(NT)]
            t_ld[j] = t_v[-1]
            t_ms = kb.emit("vector", lambda v, j=j: v.memset(vex[j][:, :], 1.0),
                           inc="ve", waits=[vex_free[j], t_q, t_k, t_v[-1]])
            t_cp = 0
            for kt in range(NT):
                t_cp = kb.emit("vector", lambda v, j=j, kt=kt: v.tensor_copy(
                    vex[j][:, kt * 65:kt * 65 + 64], vh[j][:, kt * 64:(kt + 1) * 64]),
                    inc="ve", waits=([("ve", t_ms)] if kt == 0 else []))
            t_vex[j] = t_cp
        for qc in range(8):
            exp_toks = [0] * NT
            for kt in range(NT + 2):
                if kt < NT:
                    pi = kt % 2
                    for j in range(2):
                        kb.emit("tensor", lambda t, j=j, kt=kt, qc=qc, pi=pi, pb_=pb_: t.matmul(
                            pS[pi][:, j * 512:(j + 1) * 512],
                            khp[pb_][j * 64:(j + 1) * 64, kt * 128:(kt + 1) * 128],
                            qhp[pb_][j * 64:(j + 1) * 64, qc * 512:(qc + 1) * 512],
                            start=True, stop=True),
                            inc=("mm" if j == 1 else None),
                            waits=([pS_free[pi]] if j == 0 else [])
                                  + ([t_q, t_k, t_ld[0], t_ld[1],
                                      ("ve", t_vex[0]), ("ve", t_vex[1])]
                                     if kt == 0 and qc == 0 else []))
                    t_st = kb.cnt["mm"]
                    exp_toks[kt] = kb.emit("scalar", lambda s, pi=pi: s.activation(
                        pt2[pi][:, :], pS[pi][:, :], AF.Exp, scale=0.125),
                        inc="sc", waits=[("mm", t_st), pt_free[pi]])
                    pS_free[pi] = ("sc", exp_toks[kt])
                if kt >= 2:
                    ko = kt - 2
                    pi = ko % 2
                    for j in range(2):
                        kb.emit("tensor", lambda t, j=j, ko=ko, pi=pi: t.matmul(
                            pO[j][0:65, :], vex[j][:, ko * 65:(ko + 1) * 65],
                            pt2[pi][:, j * 512:(j + 1) * 512],
                            start=(ko == 0), stop=(ko == NT - 1)),
                            inc=("mm" if j == 1 else None),
                            waits=([("sc", exp_toks[ko])] if j == 0 else [])
                                  + ([pO_free[j]] if ko == 0 else []))
                    pt_free[pi] = ("mm", kb.cnt["mm"])
                yield
            t_av = kb.cnt["mm"]
            for j in range(2):
                h = hp * 2 + j
                t_cpo = kb.emit("vector", lambda v, j=j: v.tensor_copy(oc[j][:, :], pO[j][0:65, :]),
                                inc="ve", waits=[("mm", t_av), oc_free[j]])
                t_rs = kb.emit("vector", lambda v, j=j: v.reciprocal(rs[j][:, :], oc[j][64:65, :]),
                               inc="ve", waits=[("ve", t_cpo), rs_free[j]])
                t_bc = kb.emit("tensor", lambda t, j=j: t.matmul(
                    pO[j][0:64, :], onesr[:, :], rs[j][0:1, :], start=True, stop=True),
                    inc="mm", waits=[("ve", t_rs), ("ve", t_or)])
                rs_free[j] = ("mm", t_bc)
                t_om = kb.emit("vector", lambda v, j=j: v.tensor_tensor(
                    obf[j][:, :], oc[j][0:64, :], pO[j][0:64, :], op=ALU.mult),
                    inc="ve", waits=[("mm", t_bc), obf_dma[j]])
                pO_free[j] = ("ve", t_om)
                oc_free[j] = ("ve", t_om)
                t_fl = kb.flush("vector")
                obf_dma[j] = kb.dma("sync",
                                    io["oT_s"][h * 64:(h + 1) * 64, qc * 512:(qc + 1) * 512],
                                    obf[j][:, :], waits=[("ve", t_fl)], ch=f"ch_Eo{j}")
                yield
        for j in range(2):
            hload_free[j] = ("mm", kb.cnt["mm"])
            vex_free[j] = ("mm", kb.cnt["mm"])


def _e3(kb, io):
    wob = kb.sb("E3_wob", [128, 8 * 1024], BF16)
    bob = kb.sb("E3_bob", [128, 1024], F32)
    borow = kb.sb("E3_borow", [1, 1024], F32)
    onesr3 = kb.sb("E3_onesr", [1, 128], F32)
    ot = [kb.sb(f"E3_ot{j}", [128, 8 * 128], BF16) for j in range(2)]
    sstg = [kb.sb(f"E3_sstg{j}", [128, 1024], F32) for j in range(2)]
    pj3 = [kb.ps(f"E3_pj{j}", [128, 512], F32) for j in range(2)]

    for ct in range(NCT):
        t_wo = kb.dma("gpsimd", wob[:, ct * 1024:(ct + 1) * 1024],
                      io["l1wo"][ct * 128:(ct + 1) * 128, :], ch="ch_E3w")
    t_bo = kb.dma("sync", borow[:, :], io["borow"][:, :], ch="ch_E3b")
    t_or3 = kb.emit("vector", lambda v: v.memset(onesr3[:, :], 1.0), inc="ve")
    t_ev = 0
    for j in range(2):
        t_mm = kb.emit("tensor", lambda t, j=j: t.matmul(
            pj3[0][:, :], onesr3[:, :], borow[0:1, j * 512:(j + 1) * 512], start=True, stop=True),
            inc="mm", waits=[("ve", t_or3), t_bo, ("ve", t_ev)])
        t_ev = kb.emit("vector", lambda v, j=j: v.tensor_copy(bob[:, j * 512:(j + 1) * 512],
                                                              pj3[0][:, :]),
                       inc="ve", waits=[("mm", t_mm)])
    tok_bob = t_ev
    pj3_free = [("ve", tok_bob), ("ve", 0)]
    ot_free = [("mm", 0), ("mm", 0)]
    s_dma = [("ch_Es10", 0), ("ch_Es11", 0)]
    for i in range(NT):
        ib = i % 2
        t_ot = [kb.dma("sync", ot[ib][:, ct * 128:(ct + 1) * 128],
                       io["oT_s"][ct * 128:(ct + 1) * 128, i * 128:(i + 1) * 128],
                       waits=([ot_free[ib]] if ct == 0 else []), ch=f"ch_Eot{ib}")
                for ct in range(NCT)]
        t_se = 0
        for dch in range(2):
            bank = dch % 2
            for j in range(NCT):
                kb.emit("tensor", lambda t, j=j, dch=dch, bank=bank, ib=ib: t.matmul(
                    pj3[bank][:, :], ot[ib][:, j * 128:(j + 1) * 128],
                    wob[:, j * 1024 + dch * 512: j * 1024 + (dch + 1) * 512],
                    start=(j == 0), stop=(j == NCT - 1)),
                    inc=("mm" if j == NCT - 1 else None),
                    waits=([t_ot[-1], t_wo, pj3_free[bank], s_dma[ib]] if j == 0 else []))
            tok = kb.cnt["mm"]
            t_se = kb.emit("vector", lambda v, dch=dch, bank=bank, ib=ib: v.tensor_tensor(
                sstg[ib][:, dch * 512:(dch + 1) * 512], bob[:, dch * 512:(dch + 1) * 512],
                pj3[bank][:, :], op=ALU.add),
                inc="ve", waits=[("mm", tok), ("ve", tok_bob)])
            pj3_free[bank] = ("ve", t_se)
        ot_free[ib] = ("mm", kb.cnt["mm"])
        t_fl = kb.flush("vector")
        s_dma[ib] = kb.dma("sync", io["s1_s"][i * 128:(i + 1) * 128, :], sstg[ib][:, :],
                           waits=[("ve", t_fl)], ch=f"ch_Es1{ib}")
        yield


def gen_E(kb, io):
    kb.begin_phase()
    yield from _e1(kb, io)
    g1 = [("mm", kb.cnt["mm"]), ("ve", kb.cnt["ve"]),
          ("ch_Est0", kb.cnt.get("ch_Est0", 0)), ("ch_Est1", kb.cnt.get("ch_Est1", 0))]
    kb.end_phase()
    kb.begin_phase()
    for e in KB.ENGINES:
        kb.wait(e, g1)
    yield from _e2(kb, io)
    g2 = [("mm", kb.cnt["mm"]), ("ve", kb.cnt["ve"]),
          ("ch_Eo0", kb.cnt.get("ch_Eo0", 0)), ("ch_Eo1", kb.cnt.get("ch_Eo1", 0))]
    kb.end_phase()
    kb.begin_phase()
    for e in KB.ENGINES:
        kb.wait(e, g2)
    yield from _e3(kb, io)
    kb.end_phase()



def phase_G_alloc(kb, io):
    g = {}
    g["w1q"] = [kb.sb(f"G_w1q{j}", [128, 8 * 1024], BF16) for j in range(2)]
    g["w3q"] = [kb.sb(f"G_w3q{j}", [128, 8 * 1024], BF16) for j in range(2)]
    g["w2q"] = [kb.sb(f"G_w2q{j}", [128, 8 * 1024], BF16) for j in range(2)]
    g["xq"] = [kb.sb(f"G_xq{j}", [128, 8 * 512], BF16) for j in range(2)]
    g["h1s"] = [kb.sb(f"G_h1s{j}", [128, 512], BF16) for j in range(2)]
    g["hq"] = kb.sb("G_hq", [128, 8 * 512], BF16)
    g["gstg"] = [kb.sb(f"G_gstg{j}", [128, 1024], F32) for j in range(2)]
    g["ph"] = [kb.ps(f"G_ph{j}", [128, 512], F32) for j in range(2)]

    def load_slab(p, waits_mm):
        e, fq = divmod(p, 4)
        sl = p % 2
        for ct in range(NCT):
            kb.dma("gpsimd", g["w1q"][sl][:, ct * 1024:(ct + 1) * 1024],
                   io["l1w1"][e * D + ct * 128: e * D + (ct + 1) * 128,
                              fq * 1024:(fq + 1) * 1024],
                   waits=([("mm", waits_mm)] if ct == 0 else []), ch="ch_Gw")
        for ct in range(NCT):
            kb.dma("gpsimd", g["w3q"][sl][:, ct * 1024:(ct + 1) * 1024],
                   io["l1w3"][e * D + ct * 128: e * D + (ct + 1) * 128,
                              fq * 1024:(fq + 1) * 1024], ch="ch_Gw")
        last = None
        for ft in range(NCT):
            last = kb.dma("gpsimd", g["w2q"][sl][:, ft * 1024:(ft + 1) * 1024],
                          io["l1w2"][e * FF + fq * 1024 + ft * 128:
                                     e * FF + fq * 1024 + (ft + 1) * 128, :],
                          ch="ch_Gw")
        return last

    def load_xq(p, tc):
        xb = (p * 8 + tc) % 2
        return [kb.dma("sync", g["xq"][xb][:, ct * 512:(ct + 1) * 512],
                       io["xn1T_s"][ct * 128:(ct + 1) * 128, tc * 512:(tc + 1) * 512],
                       waits=([("mm", g["xq_free"][xb])] if ct == 0 else []), ch=f"ch_Gxq{xb}")
                for ct in range(NCT)][-1]

    g["load_slab"] = load_slab
    g["load_xq"] = load_xq
    g["slab_used_mm"] = [0, 0]
    g["slab_tok"] = [None, None]
    g["xq_free"] = [0, 0]
    g["xq_tok"] = [None, None]
    g["bank_free"] = [("ve", 0), ("ve", 0)]
    g["gel_prev"] = [0, 0]
    g["gs_free"] = [("ve", 0), ("ve", 0)]
    g["slab_tok"][0] = load_slab(0, 0)
    g["xq_tok"][0] = load_xq(0, 0)
    return g


def _g_pass(kb, io, g, p, final):
    NPASS = NE * 4
    e, fq = divmod(p, 4)
    sl = p % 2
    if p + 1 < NPASS:
        g["slab_tok"][(p + 1) % 2] = g["load_slab"](p + 1, g["slab_used_mm"][(p + 1) % 2])
    for tc in range(8):
        xb = (p * 8 + tc) % 2
        t_xql = g["xq_tok"][xb]
        nxt = p * 8 + tc + 1
        if nxt < NPASS * 8:
            g["xq_tok"][nxt % 2] = g["load_xq"](nxt // 8, nxt % 8)
        t_hl = 0
        for ft in range(NCT):
            par = ft % 2
            for j in range(NCT):
                kb.emit("tensor", lambda t, j=j, ft=ft, sl=sl, xb=xb: t.matmul(
                    g["ph"][0][:, :],
                    g["w1q"][sl][:, j * 1024 + ft * 128: j * 1024 + (ft + 1) * 128],
                    g["xq"][xb][:, j * 512:(j + 1) * 512],
                    start=(j == 0), stop=(j == NCT - 1)),
                    inc=("mm" if j == NCT - 1 else None),
                    waits=([t_xql, g["slab_tok"][sl], g["bank_free"][0]] if j == 0 else []))
            tokh1 = kb.cnt["mm"]
            t_si = kb.emit("scalar", lambda s, par=par: s.activation(
                g["h1s"][par][:, :], g["ph"][0][:, :], AF.Silu),
                inc="sc", waits=[("mm", tokh1), ("ve", g["gel_prev"][par])])
            g["bank_free"][0] = ("sc", t_si)
            for j in range(NCT):
                kb.emit("tensor", lambda t, j=j, ft=ft, sl=sl, xb=xb: t.matmul(
                    g["ph"][1][:, :],
                    g["w3q"][sl][:, j * 1024 + ft * 128: j * 1024 + (ft + 1) * 128],
                    g["xq"][xb][:, j * 512:(j + 1) * 512],
                    start=(j == 0), stop=(j == NCT - 1)),
                    inc=("mm" if j == NCT - 1 else None),
                    waits=([g["bank_free"][1]] if j == 0 else []))
            tokh3 = kb.cnt["mm"]
            t_h = kb.emit("vector", lambda v, ft=ft, par=par: v.tensor_tensor(
                g["hq"][:, ft * 512:(ft + 1) * 512], g["h1s"][par][:, :], g["ph"][1][:, :],
                op=ALU.mult),
                inc="ve", waits=[("mm", tokh3), ("sc", t_si)])
            g["bank_free"][1] = ("ve", t_h)
            g["gel_prev"][par] = t_h
            t_hl = t_h
            yield
        for m in range(4):
            i = tc * 4 + m
            bb = m % 2
            t_ge = 0
            for dch in range(2):
                bank = dch % 2
                for ft in range(NCT):
                    kb.emit("tensor", lambda t, ft=ft, m=m, dch=dch, bank=bank, sl=sl: t.matmul(
                        g["ph"][bank][:, :],
                        g["hq"][:, ft * 512 + m * 128: ft * 512 + (m + 1) * 128],
                        g["w2q"][sl][:, ft * 1024 + dch * 512: ft * 1024 + (dch + 1) * 512],
                        start=(ft == 0), stop=(ft == NCT - 1)),
                        inc=("mm" if ft == NCT - 1 else None),
                        waits=([("ve", t_hl), g["bank_free"][bank]] if ft == 0 else []))
                tok = kb.cnt["mm"]
                t_ge = kb.emit("vector", lambda v, i=i, e=e, dch=dch, bank=bank, bb=bb: v.tensor_scalar(
                    g["gstg"][bb][:, dch * 512:(dch + 1) * 512], g["ph"][bank][:, :],
                    io["comb_sb"][:, i * 8 + e: i * 8 + e + 1], None, op0=ALU.mult),
                    inc="ve", waits=[("mm", tok), ("ve", io["tok_comb"]), g["gs_free"][bb]])
                g["bank_free"][bank] = ("ve", t_ge)
            if not final:
                t_fl = kb.flush("vector")
                kb.dma("gpsimd", io["gacc_s"][i * 128:(i + 1) * 128, :], g["gstg"][bb][:, :],
                       waits=[("ve", t_fl)],
                       **({"accum_op": ALU.add} if p > 0 else {}), ch="ch_Gacc")
                g["gs_free"][bb] = ("ch_Gacc", kb.cnt["ch_Gacc"])
                yield
            else:
                ep = g["ep"]
                t_gl = kb.dma("sync", ep["gl"][bb][:, :], io["gacc_s"][i * 128:(i + 1) * 128, :],
                              waits=[ep["gl_free"][bb], ("ch_Gacc", kb.cnt.get("ch_Gacc", 0))],
                              ch=f"ch_Ggl{bb}")
                t_x1 = kb.dma("sync", ep["x1t"][bb][:, :], io["x1_s"][i * 128:(i + 1) * 128, :],
                              waits=[ep["x1_free"][bb]], ch=f"ch_Gx1{bb}")
                t_s1 = kb.dma("sync", ep["s1t"][bb][:, :], io["s1_s"][i * 128:(i + 1) * 128, :],
                              waits=[ep["s1_free"][bb]], ch=f"ch_Gs1{bb}")
                t1 = kb.emit("vector", lambda v, bb=bb, ep=ep: v.tensor_tensor(
                    ep["ga"][:, :], g["gstg"][bb][:, :], ep["gl"][bb][:, :], op=ALU.add),
                    inc="ve", waits=[("ve", t_ge), t_gl, ("sc", ep["sc"])])
                g["gs_free"][bb] = ("ve", t1)
                ep["gl_free"][bb] = ("ve", t1)
                t2 = kb.emit("scalar", lambda s, ep=ep: s.activation(
                    ep["gb"][:, :], ep["ga"][:, :], AF.Silu),
                    inc="sc", waits=[("ve", t1)])
                ep["sc"] = t2
                t3 = kb.emit("vector", lambda v, bb=bb, ep=ep: v.tensor_tensor(
                    ep["ga"][:, :], ep["s1t"][bb][:, :], ep["gb"][:, :], op=ALU.mult),
                    inc="ve", waits=[("sc", t2), t_s1])
                ep["s1_free"][bb] = ("ve", t3)
                t4 = kb.emit("vector", lambda v, bb=bb, ep=ep: v.tensor_tensor(
                    ep["x2s"][bb][:, :], ep["x1t"][bb][:, :], ep["ga"][:, :], op=ALU.add),
                    inc="ve", waits=[("ve", t3), t_x1, ep["x2_free"][bb]])
                ep["x1_free"][bb] = ("ve", t4)
                t_fl = kb.flush("vector")
                ep["x2_free"][bb] = kb.dma("sync", io["out"][i * 128:(i + 1) * 128, :],
                                           ep["x2s"][bb][:, :], waits=[("ve", t_fl)],
                                           ch=f"ch_Gout{bb}")
        g["slab_used_mm"][sl] = kb.cnt["mm"]
        g["xq_free"][xb] = kb.cnt["mm"]


def gen_G_main(kb, io, g):
    for p in range(NE * 4 - 1):
        yield from _g_pass(kb, io, g, p, final=False)


def phase_G_final(kb, io, g):
    kb.begin_phase()
    ep = {}
    ep["gl"] = [kb.sb(f"G_gl{j}", [128, 1024], F32) for j in range(2)]
    ep["x1t"] = [kb.sb(f"G_x1t{j}", [128, 1024], F32) for j in range(2)]
    ep["s1t"] = [kb.sb(f"G_s1t{j}", [128, 1024], F32) for j in range(2)]
    ep["ga"] = kb.sb("G_ga", [128, 1024], F32)
    ep["gb"] = kb.sb("G_gb", [128, 1024], F32)
    ep["x2s"] = [kb.sb(f"G_x2s{j}", [128, 1024], F32) for j in range(2)]
    ep["gl_free"] = [("ve", 0), ("ve", 0)]
    ep["x1_free"] = [("ve", 0), ("ve", 0)]
    ep["s1_free"] = [("ve", 0), ("ve", 0)]
    ep["x2_free"] = [("ch_Gout0", 0), ("ch_Gout1", 0)]
    ep["sc"] = 0
    g["ep"] = ep
    for _ in _g_pass(kb, io, g, NE * 4 - 1, final=True):
        pass
    kb.end_phase()



def host_tables():
    chan = np.arange(D, dtype=np.float64)
    theta = 10000.0 ** (-2.0 * chan / D)
    t_idx = np.arange(T, dtype=np.float64)
    ang = np.outer(theta, t_idx)
    cosT = np.cos(ang)
    sinT = np.sin(ang)

    def lay(tab):
        tt = tab.reshape(NCT, 128, NT, 128)
        tt = np.transpose(tt, (1, 2, 0, 3))
        return np.ascontiguousarray(tt.reshape(128, NT * NCT * 128).astype(np.float32))

    u = np.triu(np.ones((128, 128), dtype=np.float32))
    ident = np.eye(128, dtype=ml_dtypes.bfloat16)
    return lay(cosT), lay(sinT), u, ident



def build_program(phases=("A", "B"), debug_outputs=()):
    nc = bass.Bass("TRN2", target_bir_lowering=False)
    io = {}
    if "G" in phases:
        io["out"] = nc.dram_tensor("out", [T, D], F32, kind="ExternalOutput").ap()

    def ext(name, shape, dtype):
        io[name] = nc.dram_tensor(name, shape, dtype, kind="ExternalInput").ap()

    ext("x0", [T, D], F32)
    ext("wr_row", [1, D], F32)
    ext("ident", [128, 128], BF16)
    ext("umask", [128, 128], F32)
    ext("cosT", [128, NT * NCT * 128], F32)
    ext("sinT", [128, NT * NCT * 128], F32)
    ext("wq", [D, D], F32)
    ext("wk", [D, D], F32)
    ext("wv", [D, D], F32)
    ext("wo", [D, D], F32)
    ext("wg1", [D, HD], F32)
    ext("wg2", [HD, D], F32)
    ext("nw8", [128, 8], F32)
    ext("W1", [D, FF], F32)
    ext("l1wq", [D, D], F32)
    ext("l1wk", [D, D], F32)
    ext("l1wv", [D, D], F32)
    ext("l1wo", [D, D], F32)
    ext("borow", [1, D], F32)
    ext("l1w1", [NE * D, FF], F32)
    ext("l1w3", [NE * D, FF], F32)
    ext("l1w2", [NE * FF, D], F32)
    ext("W2", [FF, D], F32)
    ext("b1r", [128, 32], F32)
    ext("b2row", [1, D], F32)
    ext("gwT", [1, NE * D], F32)
    ext("gbrow", [1, NE], F32)

    def scratch(name, shape, dtype):
        kind = "ExternalOutput" if name in debug_outputs else "Internal"
        io[name] = nc.dram_tensor(name, shape, dtype, kind=kind).ap()

    scratch("xn0T_s", [D, T], BF16)
    scratch("xn0f_s", [T, D], F32)
    scratch("s0_s", [T, D], F32)
    scratch("ofh_s", [T, D], F32)
    scratch("QT_s", [D, T], BF16)
    scratch("KT_s", [D, T], BF16)
    scratch("V_s", [T, D], BF16)
    scratch("oT_s", [D, T], BF16)
    scratch("s1_s", [T, D], F32)
    scratch("gacc_s", [T, D], F32)
    if "ypre_s" in debug_outputs:
        io["ypre_s"] = nc.dram_tensor("ypre_s", [T, D], F32, kind="ExternalOutput").ap()
    scratch("x1_s", [T, D], F32)
    scratch("xn1T_s", [D, T], BF16)

    kb = KB(nc)
    for s in ("mm", "ve", "sc", "gp", "lw"):
        kb.semaphore(s)

    io["ident_sb"] = kb.sb("identsb", [128, 128], BF16)
    t_id = kb.dma("sync", io["ident_sb"][:, :], io["ident"][:, :], ch="ch_id")
    kb.wait("tensor", [t_id])
    io["w_sb"] = kb.sb("w_sb", [128, NT], F32)
    io["scores_sb"] = kb.sb("A_scores", [128, NT], F32)
    io["comb_sb"] = kb.sb("comb_sb", [128, NT * 8], F32)
    io["tok_w"] = 0

    if "A" in phases:
        kb.begin_phase()
        phase_A(kb, io)
        kb.end_phase()
        io["tok_xn0_st"] = ("ch_Astg11", kb.cnt.get("ch_Astg11", 0))
        if "w_dump" in debug_outputs:
            io["w_dump"] = nc.dram_tensor("w_dump", [128, NT], F32, kind="ExternalOutput").ap()
            io["scores_dump"] = nc.dram_tensor("scores_dump", [128, NT], F32,
                                               kind="ExternalOutput").ap()
            t_fl = kb.flush("vector")
            kb.dma("sync", io["w_dump"][:, :], io["w_sb"][:, :],
                   waits=[("ve", t_fl)], ch="ch_dump")
            kb.dma("sync", io["scores_dump"][:, :], io["scores_sb"][:, :],
                   waits=[("ve", t_fl)], ch="ch_dump")

    if "B" in phases:
        kb.barrier()
        dbgB = "d_ek" in debug_outputs
        if dbgB:
            for nm, shp, dt_ in [("d_ek", [128, 4096], BF16), ("d_qT", [128, 4096], BF16),
                                 ("d_vg", [128, 4096], BF16), ("d_sg", [128, 4096], F32),
                                 ("d_zg", [128, 1024], F32), ("d_rz", [128, 1024], F32),
                                 ("d_kc", [128, 1024], BF16), ("d_qc", [128, 1024], BF16),
                                 ("d_kct", [128, 1024], BF16), ("d_gc", [128, 1024], F32),
                                 ("d_SCb", [128, 512], BF16)]:
                io[nm] = nc.dram_tensor(nm, shp, dt_, kind="ExternalOutput").ap()

        kb.begin_phase()
        phase_B(kb, io, dbg=dbgB)
        kb.end_phase()

    if "C" in phases:
        kb.barrier()
        kb.begin_phase()
        phase_C(kb, io)
        kb.end_phase()

    if "D" in phases:
        if "comb_dump" in debug_outputs:
            io["comb_dump"] = nc.dram_tensor("comb_dump", [128, NT * 8], F32,
                                             kind="ExternalOutput").ap()
        kb.barrier()
        kb.begin_phase()
        phase_D(kb, io)
        kb.end_phase()

    if "E" in phases and "G" in phases:
        kb.barrier()
        kb.begin_phase()
        g = phase_G_alloc(kb, io)
        ge = gen_E(kb, io)
        gg = gen_G_main(kb, io, g)
        alive_e, alive_g = True, True
        ce = cg = 0
        while alive_e or alive_g:
            if alive_e and (not alive_g or ce * 5 <= cg):
                before = kb.n_instr()
                try:
                    next(ge)
                except StopIteration:
                    alive_e = False
                ce += kb.n_instr() - before
            else:
                before = kb.n_instr()
                try:
                    next(gg)
                except StopIteration:
                    alive_g = False
                cg += kb.n_instr() - before
        kb.barrier()
        phase_G_final(kb, io, g)
        kb.end_phase()
    elif "E" in phases:
        kb.barrier()
        kb.begin_phase()
        for _ in gen_E(kb, io):
            pass
        kb.end_phase()

    final = [(s, kb.cnt[s]) for s in list(kb.sem) if s.startswith("ch_")]
    kb.wait("sync", final)
    kb.wait("gpsimd", final)
    kb.finalize()
    return nc, io


def make_in_map(inputs, core):
    cosl, sinl, u, ident = host_tables()
    nw = np.asarray(inputs["L0_attn_norm_w"], dtype=np.float32)
    m = {
        "x0": np.ascontiguousarray(np.asarray(inputs["L0_attn_x"], dtype=np.float32)[core]),
        "wr_row": np.ascontiguousarray(np.asarray(inputs["L0_ffn_w_router"], dtype=np.float32)[None, :]),
        "ident": ident,
        "umask": u,
        "cosT": cosl,
        "sinT": sinl,
        "wq": np.ascontiguousarray(np.asarray(inputs["L0_attn_wq"], dtype=np.float32)),
        "wk": np.ascontiguousarray(np.asarray(inputs["L0_attn_wk"], dtype=np.float32)),
        "wv": np.ascontiguousarray(np.asarray(inputs["L0_attn_wv"], dtype=np.float32)),
        "wo": np.ascontiguousarray(np.asarray(inputs["L0_attn_wo"], dtype=np.float32)),
        "wg1": np.ascontiguousarray(np.asarray(inputs["L0_attn_wg1"], dtype=np.float32)),
        "wg2": np.ascontiguousarray(np.asarray(inputs["L0_attn_wg2"], dtype=np.float32)),
        "nw8": np.ascontiguousarray(nw.reshape(8, 128).T),
        "W1": np.ascontiguousarray(np.asarray(inputs["L0_ffn_W1"], dtype=np.float32)),
        "W2": np.ascontiguousarray(np.asarray(inputs["L0_ffn_W2"], dtype=np.float32)),
        "b1r": np.ascontiguousarray(np.asarray(inputs["L0_ffn_b1"], dtype=np.float32).reshape(32, 128).T),
        "b2row": np.ascontiguousarray(np.asarray(inputs["L0_ffn_b2"], dtype=np.float32)[None, :]),
        "gwT": np.ascontiguousarray(np.asarray(inputs["L1_ffn_gate_w"], dtype=np.float32).T.reshape(1, -1)),
        "gbrow": np.ascontiguousarray(np.asarray(inputs["L1_ffn_gate_b"], dtype=np.float32)[None, :]),
        "l1wq": np.ascontiguousarray(np.asarray(inputs["L1_attn_Wq"], dtype=np.float32)),
        "l1wk": np.ascontiguousarray(np.asarray(inputs["L1_attn_Wk"], dtype=np.float32)),
        "l1wv": np.ascontiguousarray(np.asarray(inputs["L1_attn_Wv"], dtype=np.float32)),
        "l1wo": np.ascontiguousarray(np.asarray(inputs["L1_attn_Wo"], dtype=np.float32)),
        "borow": np.ascontiguousarray(np.asarray(inputs["L1_attn_bo"], dtype=np.float32)[None, :]),
        "l1w1": np.ascontiguousarray(np.asarray(inputs["L1_ffn_w1"], dtype=np.float32).reshape(NE * D, FF)),
        "l1w3": np.ascontiguousarray(np.asarray(inputs["L1_ffn_w3"], dtype=np.float32).reshape(NE * D, FF)),
        "l1w2": np.ascontiguousarray(np.asarray(inputs["L1_ffn_w2"], dtype=np.float32).reshape(NE * FF, D)),
    }
    return m


_NC_CACHE = {}


def kernel(**inputs):
    key = "full"
    if key not in _NC_CACHE:
        _NC_CACHE[key] = build_program(phases=("A", "B", "C", "D", "E", "G"))
    nc, io = _NC_CACHE[key]
    in_maps = [make_in_map(inputs, c) for c in range(8)]
    res = run_bass_kernel_spmd(nc, in_maps, core_ids=list(range(8)))
    out = np.stack([np.asarray(res.results[c]["out"], dtype=np.float32) for c in range(8)],
                   axis=0)
    return out
